# version 02
# baseline (speedup 1.0000x reference)
import numpy as np
from contextlib import ExitStack

import concourse.bass as bass
import concourse.mybir as mybir
from concourse.bass_utils import run_bass_kernel_spmd

F32 = mybir.dt.float32
BF16 = mybir.dt.bfloat16
I32 = mybir.dt.int32
AF = mybir.ActivationFunctionType
OP = mybir.AluOpType
AX = mybir.AxisListType

B, L, D = 4, 8192, 2048
NC = 8
TLOC = 4096
DC = D // 128
EPS = 1e-5
RA, RF = 8, 32
E, I1, IS, TOPK = 16, 1408, 2816, 6
NJ, NJS = I1 // 128, IS // 128
TG = 1024
NG = TLOC // TG
NT = TG // 512
RDT, NS = 128, 16
TCH = 512
NCH = L // TCH
DTI = 8
MAGIC = 0x51AB0000
SET_LNEXP = 6
SET_SILU = 18
FLAG_CARRY, FLAG_XN, FLAG_Y = 0, 1, 2


class Sem:
    def __init__(self, es, nc, name):
        self.h = es.enter_context(nc.semaphore(name))
        self.n = 0

    def inc(self, instr, k=1):
        instr.then_inc(self.h, k)
        self.n += k
        return self.n


def load_act_set(nc, idx):
    return nc.scalar.add_instruction(mybir.InstLoadActFuncSet(
        name=nc.get_next_instruction_name(), act_func_set_id=idx, ins=[], outs=[]))


def fm(dram_ap, c0, ncols):
    return dram_ap[:, c0:c0 + ncols].rearrange("(k p) t -> p k t", p=128)


def build(upto="full", dbg=False):
    nc = bass.Bass("TRN2", target_bir_lowering=False, detect_race_conditions=False)
    es = ExitStack()
    EE = es.enter_context

    def din(name, shape, dtype=F32):
        return nc.dram_tensor(name, list(shape), dtype, kind="ExternalInput").ap()

    xT = din("xT", (D, TLOC))
    meta = din("meta", (1, 8), I32)
    pmask8 = din("pmask8", (RA, 1))
    gam = din("gam", (128, DC))
    bet = din("bet", (128, DC))
    winA = din("winA", (DC, 128, RA))
    b_in = din("b_in", (RA, 1))
    dec_l = din("dec_l", (RA, 1))
    woutA = din("woutA", (RA + 1, D))
    dskip = din("dskip", (128, DC))
    wdnA = din("wdnA", (DC, 128, RF))
    b_dn = din("b_dn", (RF, 1))
    wupA = din("wupA", (RF + 1, D))
    xpwT = din("xpwT", (DC, 128, 160))
    dtwT = din("dtwT", (RDT, D))
    dtbT = din("dtbT", (128, DC))
    alog = din("alog", (D, NS))
    dparm = din("dparm", (128, DC))
    gwT = din("gwT", (DC, 128, E))
    w1t = din("w1t", (E, NJ, 128, DC, 256))
    w2t = din("w2t", (E, DC, 128, NJ, 128))
    sw1t = din("sw1t", (NJS, 128, DC, 256))
    sw2t = din("sw2t", (DC, 128, NJS, 128))

    outT = nc.dram_tensor("outT", [D, TLOC], F32, kind="ExternalOutput").ap()

    x1d = nc.dram_tensor("x1d", [D, TLOC], F32).ap()
    x2d = nc.dram_tensor("x2d", [D, TLOC], F32).ap()
    x3d = nc.dram_tensor("x3d", [D, TLOC], F32).ap()
    xn3d = nc.dram_tensor("xn3d", [D, TLOC], F32).ap()
    stg1 = nc.dram_tensor("stg1", [D, TLOC], F32).ap()
    stg2 = nc.dram_tensor("stg2", [D, TLOC], F32).ap()
    w1b = nc.dram_tensor("w1b", [E, NJ, 128, DC, 256], BF16).ap()
    w2b = nc.dram_tensor("w2b", [E, DC, 128, NJ, 128], BF16).ap()
    sw1b = nc.dram_tensor("sw1b", [NJS, 128, DC, 256], BF16).ap()
    cmbd = nc.dram_tensor("cmbd", [E, TLOC], F32).ap()
    xp_d = nc.dram_tensor("xp_d", [160, TLOC], F32).ap()
    xloc = nc.dram_tensor("xloc", [1024, L], F32).ap()
    yloc1 = nc.dram_tensor("yloc1", [1024, L], F32).ap()
    yloc2 = nc.dram_tensor("yloc2", [2048, TLOC], F32).ap()
    sw2b = nc.dram_tensor("sw2b", [DC, 128, NJS, 128], BF16).ap()

    shm_xn = nc.dram_tensor("shm_xn", [2, D, TLOC], F32, addr_space="Shared").ap()
    shm_xp = nc.dram_tensor("shm_xp", [2, 160, TLOC], F32, addr_space="Shared").ap()
    shm_y = nc.dram_tensor("shm_y", [2, 2, 1024, TLOC], F32, addr_space="Shared").ap()
    shm_car = nc.dram_tensor("shm_car", [2, RA], F32, addr_space="Shared").ap()
    shm_fl = nc.dram_tensor("shm_fl", [2, 8], I32, addr_space="Shared").ap()

    sGs = Sem(es, nc, "sGs")
    sGp = Sem(es, nc, "sGp")
    sGc = Sem(es, nc, "sGc")
    sGx = Sem(es, nc, "sGx")
    sGw = Sem(es, nc, "sGw")
    sQw = [Sem(es, nc, f"sQw{i}") for i in range(2)]
    sQv = [Sem(es, nc, f"sQv{i}") for i in range(2)]
    sQx = [Sem(es, nc, f"sQx{i}") for i in range(2)]
    sS = Sem(es, nc, "sS")
    sK = Sem(es, nc, "sK")
    sP = Sem(es, nc, "sP")
    sA = Sem(es, nc, "sA")
    sV = Sem(es, nc, "sV")
    sC = Sem(es, nc, "sC")

    def relay():
        SY.wait_ge(sS.h, sS.n)
        SY.sem_inc(sK.h, 1)
        sK.n += 1
        return sK.n

    SY, GP, SC, VE, TE = nc.sync, nc.gpsimd, nc.scalar, nc.vector, nc.tensor

    r_par_sy = EE(SY.register("par_sy"))
    r_par_gp = EE(GP.register("par_gp"))
    r_fl = EE(GP.register("flread"))
    r_non = EE(GP.register("nonce"))
    r_ftmp = EE(GP.register("ftmp"))
    SY.reg_load(r_par_sy, meta[0:1, 1:2])
    GP.reg_load(r_par_gp, meta[0:1, 1:2])
    GP.reg_load(r_non, meta[0:1, 3:4])
    pv_sy = SY.snap(r_par_sy, min_val=0, max_val=1)
    pv_gp = GP.snap(r_par_gp, min_val=0, max_val=1)
    nonce_sv = GP.snap(r_non, min_val=0, max_val=2**30)
    gp_xn_base = GP.snap(pv_gp * D, min_val=0, max_val=2 * D)
    gp_xp_lo = GP.snap(pv_gp * 160, min_val=0, max_val=320)
    gp_xp_hi = GP.snap(pv_gp * 160 + 128, min_val=0, max_val=320)
    gp_1024 = GP.snap(pv_gp * 1024, min_val=0, max_val=2048)
    sy_1024 = SY.snap(pv_sy * 1024, min_val=0, max_val=2048)
    sy_2048 = SY.snap(pv_sy * 2048, min_val=0, max_val=4096)

    prm = ExitStack()
    PE_ = prm.enter_context
    sb_gam = EE(nc.sbuf_tensor("sb_gam", [128, DC], F32))
    sb_bet = EE(nc.sbuf_tensor("sb_bet", [128, DC], F32))
    sb_dsk = EE(nc.sbuf_tensor("sb_dsk", [128, DC], F32))
    sb_dpm = EE(nc.sbuf_tensor("sb_dpm", [128, DC], F32))
    sb_dtb = EE(nc.sbuf_tensor("sb_dtb", [128, DTI], F32))
    sb_xpw = EE(nc.sbuf_tensor("sb_xpw", [128, DC, 160], BF16))
    sb_gw = EE(nc.sbuf_tensor("sb_gw", [128, DC, E], BF16))
    sb_1b = EE(nc.sbuf_tensor("sb_1b", [128, 1], BF16))
    sb_1f = EE(nc.sbuf_tensor("sb_1f", [1, 128], F32))
    sb_idn = EE(nc.sbuf_tensor("sb_idn", [128, 128], F32))
    sb_flv = EE(nc.sbuf_tensor("sb_flv", [1, 8], I32))
    sb_acl = EE(nc.sbuf_tensor("sb_acl", [128, DTI, NS], F32))
    sb_dtw = EE(nc.sbuf_tensor("sb_dtw", [128, DTI, 128], BF16))
    sb_low = EE(nc.sbuf_tensor("sb_low", [128, E, E], BF16))
    sb_eps = EE(nc.sbuf_tensor("sb_eps", [128, 1], F32))
    sb_win = PE_(nc.sbuf_tensor("sb_win", [128, DC, RA], F32))
    sb_wout = PE_(nc.sbuf_tensor("sb_wout", [RA + 1, D], F32))
    sb_wdn = PE_(nc.sbuf_tensor("sb_wdn", [128, DC, RF], BF16))
    sb_wup = PE_(nc.sbuf_tensor("sb_wup", [RF + 1, D], BF16))
    sb_bin = PE_(nc.sbuf_tensor("sb_bin", [RA, 1], F32))
    sb_bdn = PE_(nc.sbuf_tensor("sb_bdn", [RF, 1], F32))
    sb_dec = PE_(nc.sbuf_tensor("sb_dec", [RA, 1], F32))
    sb_pm8 = PE_(nc.sbuf_tensor("sb_pm8", [RA, 1], F32))
    sb_z8 = PE_(nc.sbuf_tensor("sb_z8", [RA, 1], F32))

    pb = [EE(nc.psum_tensor(f"pb{i}", [128, 512], F32)) for i in range(8)]

    for ap_d, ap_s in [(sb_gam, gam), (sb_bet, bet), (sb_dsk, dskip),
                       (sb_dpm, dparm)]:
        sGs.inc(GP.dma_start(out=ap_d[:, :], in_=ap_s), 16)
    sGs.inc(GP.dma_start(out=sb_bin[:, :], in_=b_in), 16)
    sGs.inc(GP.dma_start(out=sb_bdn[:, :], in_=b_dn), 16)
    sGs.inc(GP.dma_start(out=sb_pm8[:, :], in_=pmask8), 16)
    gp_8 = GP.snap(pv_gp * DTI, min_val=0, max_val=16)
    sGs.inc(GP.dma_start(out=sb_dtb[:, :], in_=dtbT[:, bass.ds(gp_8, DTI)]), 16)
    sGs.inc(GP.dma_start(out=sb_win[:, :, :], in_=winA.transpose([1, 0, 2])), 16)
    sGs.inc(GP.dma_start(out=sb_wout[:, :], in_=woutA), 16)
    sGs.inc(GP.dma_start(out=sb_wdn[:, :, :], in_=wdnA.transpose([1, 0, 2])), 16)
    sGs.inc(GP.dma_start(out=sb_wup[:, :], in_=wupA), 16)
    sGs.inc(GP.dma_start(out=sb_xpw[:, :, :], in_=xpwT.transpose([1, 0, 2])), 16)
    sGs.inc(GP.dma_start(out=sb_gw[:, :, :], in_=gwT.transpose([1, 0, 2])), 16)
    sGs.inc(GP.dma_start(out=sb_dec[:, :], in_=dec_l), 16)
    sGs.inc(GP.dma_start(
        out=sb_dtw[:, :, :],
        in_=dtwT[:, bass.ds(gp_1024, 1024)].rearrange("r (i c) -> r i c", c=128)), 16)
    sGs.inc(GP.dma_start(
        out=sb_acl[:, :, :],
        in_=alog[bass.ds(gp_1024, 1024), :].rearrange("(i p) n -> p i n", p=128)), 16)
    SETUP_TOTAL = sGs.n

    for e in range(E):
        sGp.inc(GP.dma_start(out=w1b[e], in_=w1t[e]), 16)
        sGp.inc(GP.dma_start(out=w2b[e], in_=w2t[e]), 16)
    sGp.inc(GP.dma_start(out=sw1b, in_=sw1t), 16)
    sGp.inc(GP.dma_start(out=sw2b, in_=sw2t), 16)
    PRECAST_TOTAL = sGp.n

    GP.memset(sb_idn[:, :], 0.0)
    GP.affine_select(out=sb_idn[:, :], in_=sb_idn[:, :], compare_op=OP.not_equal,
                     fill=1.0, base=0, pattern=[[-1, 128]], channel_multiplier=1)
    GP.memset(sb_low[:, :, :], 0.0)
    GP.affine_select(out=sb_low[:, :, :], in_=sb_low[:, :, :], compare_op=OP.is_ge,
                     fill=1.0, base=0, pattern=[[-1, E], [1, E]], channel_multiplier=0)
    for s in range(8):
        GP.memset(sb_flv[:, s:s + 1], MAGIC + s)
    GP.memset(sb_eps[:, :], EPS)
    nc.const_aps.register(F32, EPS, sb_eps[:, 0:1])
    c_setup = sC.inc(GP.tensor_copy(sb_flv[:, 0:1], sb_flv[:, 0:1]))
    SC.wait_ge(sC.h, c_setup)

    VE.memset(sb_1b[:, :], 1.0)
    VE.memset(sb_1f[:, :], 1.0)
    VE.memset(sb_z8[:, :], 0.0)
    v_ones = sV.inc(VE.tensor_copy(sb_1f[:, 0:1], sb_1f[:, 0:1]))

    load_act_set(nc, SET_LNEXP)
    SC.wait_ge(sGs.h, SETUP_TOTAL)
    SC.activation(sb_dec[:, :], sb_dec[:, :], AF.Exp, scale=-1.0)
    a_dec = sA.inc(SC.activation(sb_dec[:, :], sb_dec[:, :], AF.Copy, bias=1.0))
    VE.wait_ge(sA.h, a_dec)
    v_dec = sV.inc(VE.reciprocal(sb_dec[:, :], sb_dec[:, :]))
    a_acl = sA.inc(SC.activation(sb_acl[:, :, :], sb_acl[:, :, :], AF.Exp))
    VE.wait_ge(sA.h, a_acl)
    v_acl = sV.inc(VE.tensor_scalar_mul(sb_acl[:, :, :], sb_acl[:, :, :], -1.0))

    def flag_write(slot):
        GP.reg_add(r_ftmp, r_non, slot)
        with GP.If_eq(r_par_gp, 0):
            GP.reg_save(shm_fl[0:1, slot:slot + 1], r_ftmp)
        with GP.Else():
            GP.reg_save(shm_fl[1:2, slot:slot + 1], r_ftmp)

    def flag_poll(slot, rows=(0, 1)):
        for r in rows:
            GP.reg_mov(r_fl, 0)

            def cond():
                GP.reg_load(r_fl, shm_fl[r:r + 1, slot:slot + 1])
                return GP.snap(r_fl) != (nonce_sv + slot)
            with GP.While(cond):
                GP.engine_nop()
        return sC.inc(GP.tensor_copy(sb_flv[:, slot:slot + 1], sb_flv[:, slot:slot + 1]))

    def emit_ln_stats(bf_chunk, st_row, ps_sum, ps_sq, ps_mu, ps_r, sq2):
        for k in range(DC):
            TE.matmul(ps_sum[0:1, 0:512], sb_1b[:, :], bf_chunk(k),
                      start=(k == 0), stop=(k == DC - 1))
        pmk = {}
        for k in range(DC):
            if k >= 2:
                SC.wait_ge(sP.h, pmk[k - 2])
            a_sq = sA.inc(SC.activation(sq2[:, k % 2, :], bf_chunk(k), AF.Square))
            TE.wait_ge(sA.h, a_sq)
            pmk[k] = sP.inc(TE.matmul(ps_sq[0:1, 0:512], sb_1b[:, :], sq2[:, k % 2, :],
                                      start=(k == 0), stop=(k == DC - 1)))
        VE.wait_ge(sP.h, pmk[DC - 1])
        VE.tensor_scalar_mul(st_row[:, 0, :], ps_sum[0:1, 0:512], -1.0 / D)
        VE.tensor_mul(st_row[:, 1, :], st_row[:, 0, :], st_row[:, 0, :])
        v_var = sV.inc(VE.scalar_tensor_tensor(st_row[:, 1, :], ps_sq[0:1, 0:512],
                                               1.0 / D, st_row[:, 1, :],
                                               OP.mult, OP.subtract))
        SC.wait_ge(sV.h, v_var)
        SC.activation(st_row[:, 1, :], st_row[:, 1, :], AF.Ln, bias=EPS)
        a_r = sA.inc(SC.activation(st_row[:, 1, :], st_row[:, 1, :], AF.Exp, scale=-0.5))
        TE.wait_ge(sA.h, a_r)
        TE.matmul(ps_mu[:, 0:512], sb_1f[:, :], st_row[:, 0, :], start=True, stop=True)
        return sP.inc(TE.matmul(ps_r[:, 0:512], sb_1f[:, :], st_row[:, 1, :],
                                start=True, stop=True))

    pa = ExitStack()
    A = pa.enter_context
    bf_a = A(nc.sbuf_tensor("bf_a", [128, DC, 512], BF16))
    f_a = A(nc.sbuf_tensor("f_a", [128, DC, 512], F32))
    f_b = A(nc.sbuf_tensor("f_b", [128, DC, 512], F32))
    f_c = A(nc.sbuf_tensor("f_c", [128, DC, 512], F32))
    sq_a = A(nc.sbuf_tensor("sq_a", [128, 2, 512], BF16))
    st_a = A(nc.sbuf_tensor("st_a", [1, 2, 512], F32))
    u_sb = A(nc.sbuf_tensor("u_sb", [RA + 1, TLOC], F32))
    dr_a = A(nc.sbuf_tensor("dr_a", [RA, TLOC], F32))
    hc_a = A(nc.sbuf_tensor("hc_a", [RA, TLOC], F32))
    car8 = A(nc.sbuf_tensor("car8", [RA, 1], F32))
    hf_a = A(nc.sbuf_tensor("hf_a", [RF + 1, 512], BF16))

    TE.wait_ge(sV.h, v_ones)
    TE.wait_ge(sGs.h, SETUP_TOTAL)
    VE.memset(u_sb[:, :], 1.0)
    v_useed = sV.inc(VE.tensor_copy(u_sb[0:1, 0:1], u_sb[0:1, 0:1]))
    SC.wait_ge(sV.h, v_useed)
    a_u_last = 0
    for t8 in range(8):
        c0 = 512 * t8
        q_x = sQx[t8 % 2].inc(SY.dma_start(out=f_a[:, :, :], in_=fm(xT, c0, 512)), 16)
        q_x = (t8 % 2, q_x)
        GP.wait_ge(sP.h, sP.n)
        GP.wait_ge(sA.h, sA.n)
        g_x = sGx.inc(GP.dma_start(out=bf_a[:, :, :], in_=fm(xT, c0, 512)), 16)
        TE.wait_ge(sGx.h, g_x)
        SC.wait_ge(sGx.h, g_x)
        TE.wait_ge(sV.h, sV.n)
        p_bc = emit_ln_stats(lambda k: bf_a[:, k, :], st_a, pb[0], pb[1],
                             pb[2], pb[3], sq_a)
        VE.wait_ge(sP.h, p_bc)
        VE.wait_ge(sQx[q_x[0]].h, q_x[1])
        for k in range(DC):
            VE.tensor_add(f_b[:, k, :], f_a[:, k, :], pb[2][:, 0:512])
            VE.tensor_mul(f_b[:, k, :], f_b[:, k, :], pb[3][:, 0:512])
            VE.tensor_copy(bf_a[:, k, :], f_b[:, k, :])
            VE.tensor_add(f_a[:, k, :], f_a[:, k, :], f_b[:, k, :])
        v_z = sV.inc(VE.tensor_copy(st_a[:, 0, 0:1], st_a[:, 0, 0:1]))
        TE.wait_ge(sV.h, v_z)
        SC.wait_ge(sV.h, v_z)
        p_bc = emit_ln_stats(lambda k: bf_a[:, k, :], st_a, pb[0], pb[1],
                             pb[2], pb[3], sq_a)
        VE.wait_ge(sP.h, p_bc)
        for k in range(DC):
            VE.tensor_add(f_c[:, k, :], f_b[:, k, :], pb[2][:, 0:512])
            VE.tensor_mul(f_c[:, k, :], f_c[:, k, :], pb[3][:, 0:512])
            VE.tensor_scalar(f_c[:, k, :], f_c[:, k, :], sb_gam[:, k:k + 1],
                             sb_bet[:, k:k + 1], OP.mult, OP.add)
        v_xg = sV.inc(VE.tensor_copy(st_a[:, 0, 0:1], st_a[:, 0, 0:1]))
        TE.wait_ge(sV.h, v_xg)
        for k in range(DC):
            mm = TE.matmul(pb[4][0:RA, 0:512], sb_win[:, k, :], f_c[:, k, :],
                           start=(k == 0), stop=(k == DC - 1))
        p_u = sP.inc(mm)
        SC.wait_ge(sP.h, p_u)
        a_u_last = sA.inc(SC.activation(u_sb[0:RA, c0:c0 + 512], pb[4][0:RA, 0:512],
                                        AF.Identity, bias=sb_bin[:, 0:1]))
        SY.wait_ge(sV.h, v_xg)
        sS.inc(SY.dma_start(out=fm(stg1, c0, 512), in_=f_a[:, :, :]), 16)
        sS.inc(SY.dma_start(out=fm(stg2, c0, 512), in_=f_c[:, :, :]), 16)

    VE.wait_ge(sV.h, v_dec)
    VE.tensor_copy(dr_a[:, :], sb_dec[:, 0:1].broadcast_to([RA, TLOC]))
    VE.wait_ge(sA.h, a_u_last)
    if dbg:
        SY.wait_ge(sA.h, a_u_last)
        sS.inc(SY.dma_start(out=x3d[16:16 + RA, 0:TLOC], in_=u_sb[0:RA, :]), 16)
        SY.wait_ge(sS.h, sS.n)
        VE.wait_ge(sS.h, sS.n)
    VE.tensor_tensor_scan(u_sb[0:RA, :], dr_a[:, :], u_sb[0:RA, :],
                          0.0, OP.mult, OP.add)
    v_scan = sV.inc(VE.tensor_copy(st_a[:, 0, 0:1], st_a[:, 0, 0:1]))
    if dbg:
        SY.wait_ge(sV.h, v_scan)
        sS.inc(SY.dma_start(out=x3d[32:32 + RA, 0:TLOC], in_=dr_a[:, :]), 16)
        sS.inc(SY.dma_start(out=x3d[48:48 + RA, 0:TLOC], in_=u_sb[0:RA, :]), 16)
        SY.wait_ge(sS.h, sS.n)
    GP.wait_ge(sV.h, v_scan)
    GP.wait_ge(sC.h, c_setup)
    with GP.If_eq(r_par_gp, 0):
        sGc.n += 16
        GP.dma_start(out=shm_car[0:1, :].transpose([1, 0]),
                     in_=u_sb[0:RA, TLOC - 1:TLOC]).then_inc(sGc.h, 16)
    with GP.Else():
        GP.dma_start(out=shm_car[1:2, :].transpose([1, 0]),
                     in_=u_sb[0:RA, TLOC - 1:TLOC]).then_inc(sGc.h, 16)
    GP.wait_ge(sGc.h, sGc.n)
    flag_write(FLAG_CARRY)
    c_car = flag_poll(FLAG_CARRY, rows=(0,))
    g_car = sGc.inc(GP.dma_start(out=car8[:, :], in_=shm_car[0:1, :].transpose([1, 0])), 16)
    GP.wait_ge(sGc.h, g_car)
    c_mul = sC.inc(GP.tensor_tensor(car8[:, :], car8[:, :], sb_pm8[:, :], OP.mult))
    VE.wait_ge(sC.h, c_mul)
    VE.tensor_tensor_scan(hc_a[:, :], dr_a[:, :],
                          sb_z8[:, 0:1].broadcast_to([RA, TLOC]), car8[:, 0:1],
                          OP.mult, OP.add)
    VE.tensor_add(u_sb[0:RA, :], u_sb[0:RA, :], hc_a[:, :])
    v_h = sV.inc(VE.tensor_copy(st_a[:, 0, 0:1], st_a[:, 0, 0:1]))
    SY.wait_ge(sV.h, v_h)
    sS.inc(SY.dma_start(out=x3d[0:RA + 1, 0:TLOC], in_=u_sb[:, :]), 16)

    for t8 in range(8):
        c0 = 512 * t8
        sQx[t8 % 2].inc(SY.dma_start(out=f_a[:, :, :], in_=fm(stg1, c0, 512)), 16)
        q_xg = sQx[t8 % 2].inc(SY.dma_start(out=f_c[:, :, :], in_=fm(stg2, c0, 512)), 16)
        TE.wait_ge(sV.h, v_h)
        TE.wait_ge(sV.h, sV.n)
        VE.wait_ge(sQx[t8 % 2].h, q_xg)
        vmk = {}
        for k in range(DC):
            bank = pb[4 + (k % 2)]
            if k >= 2:
                TE.wait_ge(sV.h, vmk[k - 2])
            p_ar = sP.inc(TE.matmul(bank[:, 0:512], sb_wout[:, 128 * k:128 * (k + 1)],
                                    u_sb[:, c0:c0 + 512], start=True, stop=True))
            VE.wait_ge(sP.h, p_ar)
            VE.scalar_tensor_tensor(f_b[:, k, :], f_c[:, k, :], sb_dsk[:, k:k + 1],
                                    bank[:, 0:512], OP.mult, OP.add)
            vmk[k] = sV.inc(VE.tensor_add(f_b[:, k, :], f_b[:, k, :], f_a[:, k, :]))
        for k in range(DC):
            vm = VE.tensor_copy(bf_a[:, k, :], f_b[:, k, :])
        v_z1b = sV.inc(vm)
        TE.wait_ge(sV.h, v_z1b)
        SC.wait_ge(sV.h, v_z1b)
        p_bc = emit_ln_stats(lambda k: bf_a[:, k, :], st_a, pb[0], pb[1],
                             pb[2], pb[3], sq_a)
        VE.wait_ge(sP.h, p_bc)
        for k in range(DC):
            VE.tensor_add(f_c[:, k, :], f_b[:, k, :], pb[2][:, 0:512])
            VE.tensor_mul(f_c[:, k, :], f_c[:, k, :], pb[3][:, 0:512])
            vm = VE.tensor_scalar(bf_a[:, k, :], f_c[:, k, :], sb_gam[:, k:k + 1],
                                  sb_bet[:, k:k + 1], OP.mult, OP.add)
        v_xn2 = sV.inc(vm)
        TE.wait_ge(sV.h, v_xn2)
        for k in range(DC):
            mm = TE.matmul(pb[6][0:RF, 0:512], sb_wdn[:, k, :], bf_a[:, k, :],
                           start=(k == 0), stop=(k == DC - 1))
        p_hf = sP.inc(mm)
        SC.wait_ge(sP.h, p_hf)
        a_hf = sA.inc(SC.activation(hf_a[0:RF, :], pb[6][0:RF, 0:512],
                                    AF.Relu, bias=sb_bdn[:, 0:1]))
        if t8 == 0:
            v_o = sV.inc(VE.tensor_copy(hf_a[RF:RF + 1, :],
                                        sb_1b[0:1, 0:1].broadcast_to([1, 512])))
            TE.wait_ge(sV.h, v_o)
        TE.wait_ge(sA.h, a_hf)
        vmk = {}
        for k in range(DC):
            bank = pb[4 + (k % 2)]
            if k >= 2:
                TE.wait_ge(sV.h, vmk[k - 2])
            p_xf = sP.inc(TE.matmul(bank[:, 0:512], sb_wup[:, 128 * k:128 * (k + 1)],
                                    hf_a[:, :], start=True, stop=True))
            VE.wait_ge(sP.h, p_xf)
            vmk[k] = sV.inc(VE.tensor_add(f_b[:, k, :], f_b[:, k, :], bank[:, 0:512]))
        v_x1 = vmk[DC - 1]
        SY.wait_ge(sV.h, v_x1)
        sS.inc(SY.dma_start(out=fm(x1d, c0, 512), in_=f_b[:, :, :]), 16)
    pa.close()
    prm.close()

    def copy_out(src_d):
        SY.wait_ge(sS.h, sS.n)
        with nc.sbuf_tensor("cpout", [128, DC, 512], F32) as cp:
            for t8 in range(8):
                c0 = 512 * t8
                q1 = sQx[0].inc(SY.dma_start(out=cp[:, :, :], in_=fm(src_d, c0, 512)), 16)
                SY.wait_ge(sQx[0].h, q1)
                sS.inc(SY.dma_start(out=fm(outT, c0, 512), in_=cp[:, :, :]), 16)
                SY.wait_ge(sS.h, sS.n)
        GP.wait_ge(sK.h, relay())
        es.close()
        return nc

    if upto == "a":
        if dbg:
            dbg_u = nc.dram_tensor("dbg_u", [64, TLOC], F32, kind="ExternalOutput").ap()
            dbg_ac = nc.dram_tensor("dbg_ac", [D, TLOC], F32, kind="ExternalOutput").ap()
            dbg_xg = nc.dram_tensor("dbg_xg", [D, TLOC], F32, kind="ExternalOutput").ap()
            with nc.sbuf_tensor("dbg_cp", [128, DC, 512], F32) as dcp, \
                 nc.sbuf_tensor("dbg_uc", [64, TLOC], F32) as duc:
                SY.wait_ge(sV.h, sV.n)
                SY.wait_ge(sS.h, sS.n)
                for t8 in range(8):
                    c0 = 512 * t8
                    q1 = sQx[0].inc(SY.dma_start(out=dcp[:, :, :], in_=fm(stg1, c0, 512)), 16)
                    SY.wait_ge(sQx[0].h, q1)
                    sS.inc(SY.dma_start(out=fm(dbg_ac, c0, 512), in_=dcp[:, :, :]), 16)
                    SY.wait_ge(sS.h, sS.n)
                    q1 = sQx[0].inc(SY.dma_start(out=dcp[:, :, :], in_=fm(stg2, c0, 512)), 16)
                    SY.wait_ge(sQx[0].h, q1)
                    sS.inc(SY.dma_start(out=fm(dbg_xg, c0, 512), in_=dcp[:, :, :]), 16)
                    SY.wait_ge(sS.h, sS.n)
                sQx[0].inc(SY.dma_start(out=duc[:, :], in_=x3d[0:64, 0:TLOC]), 16)
                SY.wait_ge(sQx[0].h, sQx[0].n)
                sS.inc(SY.dma_start(out=dbg_u, in_=duc[:, :]), 16)
                SY.wait_ge(sS.h, sS.n)
        return copy_out(x1d)

    def emit_ffn(xin_d, xout_d, first_ffn):
        sfx = "A" if first_ffn else "B"
        pf = ExitStack()
        FA = pf.enter_context
        xn_bf = FA(nc.sbuf_tensor(f"xn_bf{sfx}", [128, DC, TG], BF16))
        y_sb = FA(nc.sbuf_tensor(f"y_sb{sfx}", [128, DC, TG], F32))
        act_bf = FA(nc.sbuf_tensor(f"act_bf{sfx}", [128, NJ, TG], BF16))
        w1sb = [FA(nc.sbuf_tensor(f"w1sb{i}{sfx}", [128, DC, 256], BF16)) for i in range(2)]
        w2sb = [FA(nc.sbuf_tensor(f"w2sb{i}{sfx}", [128, NJ, 128], BF16)) for i in range(2)]
        silu = FA(nc.sbuf_tensor(f"silu{sfx}", [128, 2, 2, 512], F32))
        xf32 = FA(nc.sbuf_tensor(f"xf32{sfx}", [128, 2, 512], F32))
        sqf = FA(nc.sbuf_tensor(f"sqf{sfx}", [128, 2, 512], BF16))
        strw = FA(nc.sbuf_tensor(f"strw{sfx}", [1, 2, 512], F32))
        lgt = FA(nc.sbuf_tensor(f"lgt{sfx}", [E, TG], F32))
        ltok = FA(nc.sbuf_tensor(f"ltok{sfx}", [128, 8, E], F32))
        cpa = FA(nc.sbuf_tensor(f"cpa{sfx}", [128, 8, E, E], BF16))
        cpb = FA(nc.sbuf_tensor(f"cpb{sfx}", [128, 8, E, E], BF16))
        cntr = FA(nc.sbuf_tensor(f"cntr{sfx}", [128, 8, E], F32))
        cntb = FA(nc.sbuf_tensor(f"cntb{sfx}", [128, 8, E], F32))
        ssm = FA(nc.sbuf_tensor(f"ssm{sfx}", [128, 8, 1], F32))
        ssb = FA(nc.sbuf_tensor(f"ssb{sfx}", [128, 8, 1], F32))
        rsb = FA(nc.sbuf_tensor(f"rsb{sfx}", [128, 8, 1], F32))
        mskb = FA(nc.sbuf_tensor(f"mskb{sfx}", [128, 8, E], F32))
        cmb = FA(nc.sbuf_tensor(f"cmb{sfx}", [128, 8, E], F32))
        cmbT = FA(nc.sbuf_tensor(f"cmbT{sfx}", [E, TG], F32))
        cbc = FA(nc.sbuf_tensor(f"cbc{sfx}", [128, 2, 512], F32))
        xnf = FA(nc.sbuf_tensor(f"xnf{sfx}", [128, 2, 512], F32))
        xpo = FA(nc.sbuf_tensor(f"xpo{sfx}", [128, 512], F32))
        xpo2 = FA(nc.sbuf_tensor(f"xpo2{sfx}", [32, 512], F32))

        if first_ffn:
            SY.wait_ge(sGp.h, PRECAST_TOTAL)

        for g in range(NG):
            t0 = TG * g
            TE.wait_ge(sV.h, sV.n)
            TE.wait_ge(sA.h, sA.n)
            GP.wait_ge(sP.h, sP.n)
            GP.wait_ge(sV.h, sV.n)
            GP.wait_ge(sA.h, sA.n)
            GP.wait_ge(sK.h, relay())
            g_xb = sGx.inc(GP.dma_start(out=xn_bf[:, :, :], in_=fm(xin_d, t0, TG)), 16)
            load_act_set(nc, SET_LNEXP)
            xfmk = {}
            lidx = 0
            for tt in range(NT):
                cs = slice(512 * tt, 512 * (tt + 1))
                TE.wait_ge(sGx.h, g_xb)
                SC.wait_ge(sGx.h, g_xb)
                if tt > 0:
                    TE.wait_ge(sV.h, sV.n)
                p_bc = emit_ln_stats(lambda k: xn_bf[:, k, cs], strw,
                                     pb[4], pb[5], pb[4], pb[5], sqf)
                VE.wait_ge(sP.h, p_bc)
                for k in range(DC):
                    buf = lidx % 2
                    if lidx >= 2:
                        SY.wait_ge(sV.h, xfmk[lidx - 2])
                    q_xf = sQx[buf].inc(SY.dma_start(
                        out=xf32[:, buf, :],
                        in_=xin_d[128 * k:128 * (k + 1),
                                  t0 + 512 * tt:t0 + 512 * (tt + 1)]), 16)
                    VE.wait_ge(sQx[buf].h, q_xf)
                    VE.tensor_add(xf32[:, buf, :], xf32[:, buf, :], pb[4][:, 0:512])
                    xfmk[lidx] = sV.inc(
                        VE.tensor_mul(xn_bf[:, k, cs], xf32[:, buf, :], pb[5][:, 0:512]))
                    lidx += 1
                v_xn = xfmk[lidx - 1]
            TE.wait_ge(sV.h, v_xn)
            for tt in range(NT):
                cs = slice(512 * tt, 512 * (tt + 1))
                for k in range(DC):
                    mm = TE.matmul(pb[6][0:E, 0:512], sb_gw[:, k, :], xn_bf[:, k, cs],
                                   start=(k == 0), stop=(k == DC - 1))
                p_lg = sP.inc(mm)
                SC.wait_ge(sP.h, p_lg)
                a_lg = sA.inc(SC.activation(lgt[:, cs], pb[6][0:E, 0:512], AF.Exp))
                TE.wait_ge(sA.h, a_lg)
            for c8 in range(8):
                TE.wait_ge(sA.h, sA.n)
                p_tr = sP.inc(TE.transpose(pb[7][:, 0:E],
                                           lgt[0:E, 128 * c8:128 * (c8 + 1)],
                                           sb_idn[0:16, 0:16]))
                SC.wait_ge(sP.h, p_tr)
                sA.inc(SC.activation(ltok[:, c8, :], pb[7][:, 0:E], AF.Copy))
            VE.wait_ge(sA.h, sA.n)
            li = ltok[:, :, :].unsqueeze(3).broadcast_to([128, 8, E, E])
            lj = ltok[:, :, :].unsqueeze(2).broadcast_to([128, 8, E, E])
            VE.tensor_tensor(cpa[:, :, :, :], lj, li, OP.is_gt)
            VE.tensor_tensor(cpb[:, :, :, :], lj, li, OP.is_equal)
            VE.tensor_tensor(cpb[:, :, :, :], cpb[:, :, :, :],
                             sb_low[:, :, :].unsqueeze(1).broadcast_to([128, 8, E, E]),
                             OP.mult)
            VE.tensor_tensor(cpa[:, :, :, :], cpa[:, :, :, :], cpb[:, :, :, :], OP.add)
            VE.tensor_reduce(cntr[:, :, :].unsqueeze(3), cpa[:, :, :, :], AX.X, OP.add)
            v_ct = sV.inc(VE.tensor_reduce(ssm[:, :, :], ltok[:, :, :], AX.X, OP.add))
            GP.wait_ge(sV.h, v_ct)
            GP.tensor_copy(cntb[:, :, :], cntr[:, :, :])
            c_ct = sC.inc(GP.tensor_copy(ssb[:, :, :], ssm[:, :, :]))
            VE.wait_ge(sC.h, c_ct)
            VE.reciprocal(rsb[:, :, :], ssb[:, :, :])
            VE.tensor_single_scalar(mskb[:, :, :], cntb[:, :, :], float(TOPK) - 0.5,
                                    OP.is_lt)
            v_cu = sV.inc(VE.tensor_mul(cmb[:, :, :], ltok[:, :, :], mskb[:, :, :]))
            GP.wait_ge(sV.h, v_cu)
            c_cm = sC.inc(GP.tensor_tensor(cmb[:, :, :], cmb[:, :, :],
                                           rsb[:, :, :].broadcast_to([128, 8, E]),
                                           OP.mult))
            TE.wait_ge(sC.h, c_cm)
            for c8 in range(8):
                TE.wait_ge(sA.h, sA.n)
                p_tb = sP.inc(TE.transpose(pb[7][0:E, 0:128], cmb[:, c8, :],
                                           sb_idn[:, :]))
                SC.wait_ge(sP.h, p_tb)
                sA.inc(SC.activation(cmbT[:, 128 * c8:128 * (c8 + 1)],
                                     pb[7][0:E, 0:128], AF.Copy))
            GP.wait_ge(sA.h, sA.n)
            sGw.inc(GP.dma_start(out=cmbd[:, t0:t0 + TG], in_=cmbT[:, :]), 16)
            GP.wait_ge(sGw.h, sGw.n)
            load_act_set(nc, SET_SILU)
            iters = [("e", e, j) for e in range(E) for j in range(NJ)]
            iters += [("s", 0, j) for j in range(NJ)]
            iters += [("s", 1, j) for j in range(NJ)]
            it = 0
            st1 = {}
            prev_key = None
            cbc_mark = None
            p_y_prev_exp = 0
            st2_prev = (0, 0)
            v_act = 0
            for kind, e, j in iters:
                buf = it % 2
                njc = NJ
                if (kind, e) != prev_key:
                    if (dbg and first_ffn and g == 0 and kind == "s" and e == 0):
                        SY.wait_ge(sV.h, sV.n)
                        SY.wait_ge(sA.h, sA.n)
                        sS.inc(SY.dma_start(out=fm(x3d, 0, TG), in_=y_sb[:, :, :]), 16)
                        SY.wait_ge(sS.h, sS.n)
                    prev_key = (kind, e)
                    first_of_exp = it
                    if kind == "e":
                        GP.wait_ge(sV.h, sV.n)
                        cbc_mark = sGx.inc(GP.dma_start(
                            out=cbc[:, :, :].rearrange("p a t -> p (a t)"),
                            in_=cmbd[e:e + 1, t0:t0 + TG].broadcast_to([128, TG])), 16)
                if it >= 2:
                    SY.wait_ge(sP.h, st1[it - 2][0])
                wsrc = w1b[e, j] if kind == "e" else sw1b[NJ * e + j]
                q_w = sQw[buf].inc(SY.dma_start(out=w1sb[buf][:, :, :],
                                                in_=wsrc), 16)
                TE.wait_ge(sQw[buf].h, q_w)
                if it >= 1:
                    TE.wait_ge(sV.h, st1[it - 1][1])
                    TE.wait_ge(sA.h, st1[it - 1][2])
                for tt in range(NT):
                    cs = slice(512 * tt, 512 * (tt + 1))
                    gps, ups = pb[0 + tt], pb[2 + tt]
                    for k in range(DC):
                        TE.matmul(gps[:, 0:512], w1sb[buf][:, k, 0:128], xn_bf[:, k, cs],
                                  start=(k == 0), stop=(k == DC - 1))
                    for k in range(DC):
                        mm = TE.matmul(ups[:, 0:512], w1sb[buf][:, k, 128:256],
                                       xn_bf[:, k, cs], start=(k == 0), stop=(k == DC - 1))
                    p_gu = sP.inc(mm)
                    SC.wait_ge(sP.h, p_gu)
                    if it >= 2:
                        SC.wait_ge(sV.h, st1[it - 2][1])
                    a_si = sA.inc(SC.activation(silu[:, buf, tt, :], gps[:, 0:512],
                                                AF.Silu))
                    VE.wait_ge(sP.h, p_gu)
                    VE.wait_ge(sA.h, a_si)
                    if it == first_of_exp and tt == 0:
                        VE.wait_ge(sP.h, p_y_prev_exp)
                        if kind == "e":
                            VE.wait_ge(sGx.h, cbc_mark)
                    if kind == "e":
                        VE.tensor_mul(silu[:, buf, tt, :], silu[:, buf, tt, :],
                                      ups[:, 0:512])
                        vm = VE.tensor_mul(act_bf[:, j, cs], silu[:, buf, tt, :],
                                           cbc[:, tt, :])
                    else:
                        vm = VE.tensor_mul(act_bf[:, j, cs], silu[:, buf, tt, :],
                                           ups[:, 0:512])
                    if tt == NT - 1:
                        v_act = sV.inc(vm)
                st1[it] = (p_gu, v_act, a_si)
                it += 1
                if j == njc - 1:
                    first_e = (kind == "e" and e == 0)
                    if kind == "e":
                        w2src = (lambda ee: lambda hc: w2b[ee, hc])(e)
                    else:
                        w2src = (lambda hh: lambda hc: sw2b[hc][:, NJ * hh:NJ * (hh + 1), :])(e)
                    W2Q = SY if first_e else SC
                    st2 = {}
                    for hc in range(DC):
                        b2 = hc % 2
                        if hc >= 2:
                            W2Q.wait_ge(sP.h, st2[hc - 2][0])
                        q_w2 = sQv[b2].inc(W2Q.dma_start(out=w2sb[b2][:, 0:njc, :],
                                                         in_=w2src(hc)), 16)
                        TE.wait_ge(sQv[b2].h, q_w2)
                        if hc == 0:
                            TE.wait_ge(sV.h, v_act)
                            TE.wait_ge(sV.h, st2_prev[0])
                            TE.wait_ge(sA.h, st2_prev[1])
                        for tt in range(NT):
                            cs = slice(512 * tt, 512 * (tt + 1))
                            yps = pb[4 + tt]
                            if hc >= 1:
                                TE.wait_ge(sV.h, st2[hc - 1][1])
                                TE.wait_ge(sA.h, st2[hc - 1][2])
                            for ic in range(njc):
                                mm = TE.matmul(yps[:, 0:512], w2sb[b2][:, ic, :],
                                               act_bf[:, ic, cs],
                                               start=(ic == 0), stop=(ic == njc - 1))
                            p_y = sP.inc(mm)
                            if first_e:
                                SC.wait_ge(sP.h, p_y)
                                a_y = sA.inc(SC.activation(y_sb[:, hc, cs],
                                                           yps[:, 0:512], AF.Copy))
                                v_y = sV.n
                            else:
                                VE.wait_ge(sP.h, p_y)
                                if kind == "e" and e == 1 and hc == 0 and tt == 0:
                                    VE.wait_ge(sA.h, sA.n)
                                v_y = sV.inc(VE.tensor_add(y_sb[:, hc, cs],
                                                           y_sb[:, hc, cs], yps[:, 0:512]))
                                a_y = sA.n
                            st2[hc] = (p_y, v_y, a_y)
                    p_y_prev_exp = st2[DC - 1][0]
                    st2_prev = (st2[DC - 1][1], st2[DC - 1][2])
            v_yfin, a_yfin = sV.n, sA.n
            load_act_set(nc, SET_LNEXP)
            VE.wait_ge(sV.h, v_yfin)
            VE.wait_ge(sA.h, a_yfin)
            fmk = {}
            idx = 0
            for k in range(DC):
                for tt in range(NT):
                    buf = idx % 2
                    if idx >= 2:
                        SY.wait_ge(sV.h, fmk[idx - 2])
                    q_xf = sQx[buf].inc(SY.dma_start(
                        out=xf32[:, buf, :],
                        in_=xin_d[128 * k:128 * (k + 1),
                                  t0 + 512 * tt:t0 + 512 * (tt + 1)]), 16)
                    VE.wait_ge(sQx[buf].h, q_xf)
                    cs = slice(512 * tt, 512 * (tt + 1))
                    fmk[idx] = sV.inc(VE.tensor_add(xf32[:, buf, :], xf32[:, buf, :],
                                                    y_sb[:, k, cs]))
                    SY.wait_ge(sV.h, fmk[idx])
                    sS.inc(SY.dma_start(
                        out=xout_d[128 * k:128 * (k + 1),
                                   t0 + 512 * tt:t0 + 512 * (tt + 1)],
                        in_=xf32[:, buf, :]), 16)
                    idx += 1
            q_xout = sS.n
            if first_ffn:
                GP.wait_ge(sP.h, sP.n)
                GP.wait_ge(sV.h, sV.n)
                GP.wait_ge(sA.h, sA.n)
                GP.wait_ge(sK.h, relay())
                g_x2 = sGx.inc(GP.dma_start(out=xn_bf[:, :, :],
                                            in_=fm(xout_d, t0, TG)), 16)
                for tt in range(NT):
                    cs = slice(512 * tt, 512 * (tt + 1))
                    TE.wait_ge(sGx.h, g_x2)
                    SC.wait_ge(sGx.h, g_x2)
                    TE.wait_ge(sV.h, sV.n)
                    TE.wait_ge(sA.h, sA.n)
                    p_bc = emit_ln_stats(lambda k: xn_bf[:, k, cs], strw,
                                         pb[4], pb[5], pb[4], pb[5], sqf)
                    VE.wait_ge(sP.h, p_bc)
                    nmk = {}
                    for k in range(DC):
                        buf = (tt * DC + k) % 2
                        q_xf = sQx[buf].inc(SY.dma_start(
                            out=xf32[:, buf, :],
                            in_=xout_d[128 * k:128 * (k + 1),
                                       t0 + 512 * tt:t0 + 512 * (tt + 1)]), 16)
                        VE.wait_ge(sQx[buf].h, q_xf)
                        VE.tensor_add(xnf[:, buf, :], xf32[:, buf, :], pb[4][:, 0:512])
                        VE.tensor_mul(xnf[:, buf, :], xnf[:, buf, :], pb[5][:, 0:512])
                        nmk[k] = sV.inc(VE.tensor_copy(xn_bf[:, k, cs], xnf[:, buf, :]))
                        SY.wait_ge(sV.h, nmk[k])
                        sS.inc(SY.dma_start(
                            out=xn3d[128 * k:128 * (k + 1),
                                     t0 + 512 * tt:t0 + 512 * (tt + 1)],
                            in_=xnf[:, buf, :]), 16)

                    v_xn3 = nmk[DC - 1]
                    TE.wait_ge(sV.h, v_xn3)
                    TE.wait_ge(sA.h, sA.n)
                    for k in range(DC):
                        TE.matmul(pb[6][0:128, 0:512], sb_xpw[:, k, 0:128],
                                  xn_bf[:, k, cs], start=(k == 0), stop=(k == DC - 1))
                    for k in range(DC):
                        mm = TE.matmul(pb[7][0:32, 0:512], sb_xpw[:, k, 128:160],
                                       xn_bf[:, k, cs], start=(k == 0), stop=(k == DC - 1))
                    p_xp = sP.inc(mm)
                    SC.wait_ge(sP.h, p_xp)
                    SC.activation(xpo[:, :], pb[6][0:128, 0:512], AF.Copy)
                    a_xp = sA.inc(SC.activation(xpo2[:, :], pb[7][0:32, 0:512],
                                                AF.Copy))
                    GP.wait_ge(sA.h, a_xp)
                    sGw.inc(GP.dma_start(
                        out=xp_d[0:128, t0 + 512 * tt:t0 + 512 * (tt + 1)],
                        in_=xpo[:, :]), 16)
                    sGw.inc(GP.dma_start(
                        out=xp_d[128:160, t0 + 512 * tt:t0 + 512 * (tt + 1)],
                        in_=xpo2[:, :]), 16)
        pf.close()

    emit_ffn(x1d, x2d, first_ffn=True)
    GP.wait_ge(sGw.h, sGw.n)
    GP.wait_ge(sK.h, relay())
    sGw.inc(GP.dma_start(
        out=shm_xn.rearrange("s d t -> (s d) t")[bass.ds(gp_xn_base, D), :],
        in_=xn3d[:, :]), 16)
    sGw.inc(GP.dma_start(
        out=shm_xp.rearrange("s p t -> (s p) t")[bass.ds(gp_xp_lo, 160), :],
        in_=xp_d[:, :]), 16)
    GP.wait_ge(sGw.h, sGw.n)
    flag_write(FLAG_XN)

    if upto == "f1":
        if dbg:
            dbg_cmb = nc.dram_tensor("dbg_cmb", [E, TLOC], F32, kind="ExternalOutput").ap()
            dbg_moe = nc.dram_tensor("dbg_moe", [D, TG], F32, kind="ExternalOutput").ap()
            with nc.sbuf_tensor("dbg_cb", [E, TLOC], F32) as dcb:
                SY.wait_ge(sS.h, sS.n)
                q1 = sQx[0].inc(SY.dma_start(out=dcb[:, :], in_=cmbd), 16)
                SY.wait_ge(sQx[0].h, q1)
                sS.inc(SY.dma_start(out=dbg_cmb, in_=dcb[:, :]), 16)
                SY.wait_ge(sS.h, sS.n)
            with nc.sbuf_tensor("dbg_my", [128, DC, 512], F32) as dmy:
                for tt in range(2):
                    q1 = sQx[0].inc(SY.dma_start(out=dmy[:, :, :],
                                                 in_=fm(x3d, 512 * tt, 512)), 16)
                    SY.wait_ge(sQx[0].h, q1)
                    sS.inc(SY.dma_start(out=fm(dbg_moe, 512 * tt, 512),
                                        in_=dmy[:, :, :]), 16)
                    SY.wait_ge(sS.h, sS.n)
        return copy_out(x2d)

    c_xn = flag_poll(FLAG_XN, rows=(0, 1))
    pc = ExitStack()
    CCx = pc.enter_context
    bcb = CCx(nc.sbuf_tensor("bcb", [128, 32, TCH], F32))
    dlow = CCx(nc.sbuf_tensor("dlow", [128, 2, TCH], BF16))
    xrow = CCx(nc.sbuf_tensor("xrow", [128, 2, TCH], F32))
    dtt = CCx(nc.sbuf_tensor("dtt", [128, TCH], F32))
    spt = CCx(nc.sbuf_tensor("spt", [128, TCH], F32))
    dtx = CCx(nc.sbuf_tensor("dtx", [128, TCH], F32))
    abar = CCx(nc.sbuf_tensor("abar", [128, TCH, NS], F32))
    bxh = CCx(nc.sbuf_tensor("bxh", [128, TCH, NS], F32))
    hsc = CCx(nc.sbuf_tensor("hsc", [128, TCH, NS], F32))
    ysc = CCx(nc.sbuf_tensor("ysc", [128, 2, TCH, 1], F32))
    carr = CCx(nc.sbuf_tensor("carr", [128, DTI, NS], F32))

    SY.wait_ge(sC.h, c_xn)
    GP.wait_ge(sC.h, c_xn)
    VE.wait_ge(sC.h, c_xn)
    VE.wait_ge(sV.h, v_acl)
    for Hh in range(2):
        sQx[0].inc(SY.dma_start(
            out=xloc[:, TLOC * Hh:TLOC * (Hh + 1)],
            in_=shm_xn[Hh][bass.ds(sy_1024, 1024), :]), 16)
    q_xloc = sQx[0].n
    SY.wait_ge(sQx[0].h, q_xloc)
    load_act_set(nc, SET_LNEXP)
    ymk = {}
    carr_mark = {}
    for ch in range(NCH):
        H = ch // 8
        cc = 512 * (ch % 8)
        GP.wait_ge(sV.h, sV.n)
        GP.wait_ge(sP.h, sP.n)
        g_bc = sGx.inc(GP.dma_start(
            out=bcb[:, :, :],
            in_=shm_xp[H, 128:160, cc:cc + TCH].unsqueeze(0).broadcast_to(
                [128, 32, TCH])), 16)
        g_dl = sGx.inc(GP.dma_start(out=dlow[:, ch % 2, :],
                                    in_=shm_xp[H, 0:128, cc:cc + TCH]), 16)
        for i in range(DTI):
            dbank = pb[i % 2]
            if (ch, i) >= (0, 2):
                SY.wait_ge(sV.h, sV.n)
            q_x = sQx[i % 2].inc(SY.dma_start(
                out=xrow[:, i % 2, :],
                in_=xloc[128 * i:128 * (i + 1), TLOC * H + cc:TLOC * H + cc + TCH]), 16)
            q_x = (i % 2, q_x)
            TE.wait_ge(sGx.h, g_dl)
            TE.wait_ge(sA.h, sA.n)
            p_dt = sP.inc(TE.matmul(dbank[:, 0:512], sb_dtw[:, i, :],
                                    dlow[:, ch % 2, :], start=True, stop=True))
            SC.wait_ge(sP.h, p_dt)
            SC.wait_ge(sV.h, sV.n)
            SC.activation(spt[:, :], dbank[:, 0:512], AF.Abs, bias=sb_dtb[:, i:i + 1])
            SC.activation(spt[:, :], spt[:, :], AF.Exp, scale=-1.0)
            SC.activation(spt[:, :], spt[:, :], AF.Ln, bias=1.0)
            a_sp = sA.inc(SC.activation(dtt[:, :], dbank[:, 0:512], AF.Relu,
                                        bias=sb_dtb[:, i:i + 1]))
            VE.wait_ge(sA.h, a_sp)
            v_dt = sV.inc(VE.tensor_add(dtt[:, :], dtt[:, :], spt[:, :]))
            SC.wait_ge(sV.h, v_dt)
            for n in range(NS):
                am = SC.activation(abar[:, :, n], dtt[:, :], AF.Exp,
                                   scale=sb_acl[:, i, n:n + 1])
            a_ab = sA.inc(am)
            VE.wait_ge(sQx[q_x[0]].h, q_x[1])
            VE.tensor_mul(dtx[:, :], dtt[:, :], xrow[:, i % 2, :])
            VE.wait_ge(sGx.h, g_bc)
            VE.tensor_tensor(bxh[:, :, :],
                             dtx[:, :].unsqueeze(2).broadcast_to([128, TCH, NS]),
                             bcb[:, 0:NS, :].transpose([0, 2, 1]), OP.mult)
            VE.wait_ge(sA.h, a_ab)
            if ch > 0:
                VE.wait_ge(sC.h, carr_mark[i])
            for n in range(NS):
                init = 0.0 if ch == 0 else carr[:, i, n:n + 1]
                VE.tensor_tensor_scan(hsc[:, :, n], abar[:, :, n], bxh[:, :, n],
                                      init, OP.mult, OP.add)
            v_hs = sV.inc(VE.tensor_copy(dtt[:, 0:1], dtt[:, 0:1]))
            GP.wait_ge(sV.h, v_hs)
            carr_mark[i] = sC.inc(GP.tensor_copy(carr[:, i, :], hsc[:, TCH - 1, :]))
            VE.tensor_tensor(bxh[:, :, :], hsc[:, :, :],
                             bcb[:, NS:32, :].transpose([0, 2, 1]), OP.mult)
            ybuf2 = (ch * DTI + i) % 2
            if (ch * DTI + i) >= 2:
                VE.wait_ge(sS.h, ymk[(ch * DTI + i) - 2])
            v_y = sV.inc(VE.tensor_reduce(ysc[:, ybuf2, :, :], bxh[:, :, :], AX.X, OP.add))
            SY.wait_ge(sV.h, v_y)
            ymk[ch * DTI + i] = sS.inc(SY.dma_start(
                out=yloc1[128 * i:128 * (i + 1), TLOC * H + cc:TLOC * H + cc + TCH],
                in_=ysc[:, ybuf2, :, 0]), 16)
    pc.close()
    SY.wait_ge(sS.h, sS.n)
    for Hh in range(2):
        sS.inc(SY.dma_start(
            out=shm_y[Hh].rearrange("s d t -> (s d) t")[bass.ds(sy_1024, 1024), :],
            in_=yloc1[:, TLOC * Hh:TLOC * (Hh + 1)]), 16)
    GP.wait_ge(sK.h, relay())
    flag_write(FLAG_Y)
    c_y = flag_poll(FLAG_Y, rows=(0, 1))

    pd = ExitStack()
    DDx = pd.enter_context
    ybf = DDx(nc.sbuf_tensor("ybf", [128, 2, 512], F32))
    nbf = DDx(nc.sbuf_tensor("nbf", [128, 2, 512], F32))
    obf = DDx(nc.sbuf_tensor("obf", [128, 2, 512], F32))
    SY.wait_ge(sC.h, c_y)
    VE.wait_ge(sC.h, c_y)
    sQx[0].inc(SY.dma_start(
        out=yloc2[:, :],
        in_=shm_y.rearrange("h s d t -> (h s d) t")[bass.ds(sy_2048, 2048), :]), 16)
    SY.wait_ge(sQx[0].h, sQx[0].n)
    dmk = {}
    idx = 0
    for t8 in range(8):
        c0 = 512 * t8
        for k in range(DC):
            buf = idx % 2
            if idx >= 2:
                SY.wait_ge(sV.h, dmk[idx - 2])
            sQx[buf].inc(SY.dma_start(
                out=ybf[:, buf, :],
                in_=yloc2[1024 * (k // 8) + 128 * (k % 8):
                          1024 * (k // 8) + 128 * (k % 8) + 128, c0:c0 + 512]), 16)
            sQx[buf].inc(SY.dma_start(
                out=nbf[:, buf, :],
                in_=xn3d[128 * k:128 * (k + 1), c0:c0 + 512]), 16)
            q_x2 = sQx[buf].inc(SY.dma_start(
                out=obf[:, buf, :],
                in_=x2d[128 * k:128 * (k + 1), c0:c0 + 512]), 16)
            VE.wait_ge(sQx[buf].h, q_x2)
            VE.scalar_tensor_tensor(nbf[:, buf, :], nbf[:, buf, :],
                                    sb_dpm[:, k:k + 1], ybf[:, buf, :],
                                    OP.mult, OP.add)
            dmk[idx] = sV.inc(VE.tensor_add(obf[:, buf, :], obf[:, buf, :],
                                            nbf[:, buf, :]))
            SY.wait_ge(sV.h, dmk[idx])
            sS.inc(SY.dma_start(out=x3d[128 * k:128 * (k + 1), c0:c0 + 512],
                                in_=obf[:, buf, :]), 16)
            idx += 1
    pd.close()
    if upto == "b":
        return copy_out(x3d)

    emit_ffn(x3d, outT, first_ffn=False)

    SY.wait_ge(sS.h, sS.n)
    GP.wait_ge(sK.h, relay())
    es.close()
    return nc


def _prep(inputs):
    c = np.ascontiguousarray
    f = np.float32
    x = np.asarray(inputs["L0_attn_a_x"], f)
    xt = c(x.reshape(B * L, D).T)

    gamma = np.asarray(inputs["L0_attn_a_ln_gamma"], f)
    beta = np.asarray(inputs["L0_attn_a_ln_beta"], f)
    W_in = np.asarray(inputs["L0_attn_a_W_in"], f)
    b_in = np.asarray(inputs["L0_attn_a_b_in"], f)
    dec_l = np.asarray(inputs["L0_attn_a_decay_logit"], f)
    W_out = np.asarray(inputs["L0_attn_a_W_out"], f)
    b_out = np.asarray(inputs["L0_attn_a_b_out"], f)
    D_skip = np.asarray(inputs["L0_attn_a_D_skip"], f)
    W_down = np.asarray(inputs["L0_attn_a_W_down"], f)
    b_down = np.asarray(inputs["L0_attn_a_b_down"], f)
    W_up = np.asarray(inputs["L0_attn_a_W_up"], f)
    b_up = np.asarray(inputs["L0_attn_a_b_up"], f)
    xpw = np.asarray(inputs["L0_attn_b_x_proj_w"], f)
    dtw = np.asarray(inputs["L0_attn_b_dt_proj_w"], f)
    dtb = np.asarray(inputs["L0_attn_b_dt_proj_b"], f)
    A_log = np.asarray(inputs["L0_attn_b_A_log"], f)
    D_param = np.asarray(inputs["L0_attn_b_D_param"], f)
    gate_w = np.asarray(inputs["L0_ffn_gate_w"], f)
    w1 = np.asarray(inputs["L0_ffn_w1"], f)
    w2 = np.asarray(inputs["L0_ffn_w2"], f)
    sw1 = np.asarray(inputs["L0_ffn_shared_w1"], f)
    sw2 = np.asarray(inputs["L0_ffn_shared_w2"], f)

    cm = lambda v: c(v.reshape(DC, 128).T)

    w1v = w1.reshape(E, 2 * I1, DC, 128)
    w1g = w1v[:, :I1].reshape(E, NJ, 128, DC, 128).transpose(0, 1, 4, 3, 2)
    w1u = w1v[:, I1:].reshape(E, NJ, 128, DC, 128).transpose(0, 1, 4, 3, 2)
    w1t = c(np.concatenate([w1g, w1u], axis=-1))
    w2t = c(w2.reshape(E, DC, 128, NJ, 128).transpose(0, 1, 4, 3, 2))
    s1g = sw1[:IS].reshape(NJS, 128, DC, 128).transpose(0, 3, 2, 1)
    s1u = sw1[IS:].reshape(NJS, 128, DC, 128).transpose(0, 3, 2, 1)
    sw1t = c(np.concatenate([s1g, s1u], axis=-1))
    sw2t = c(sw2.reshape(DC, 128, NJS, 128).transpose(0, 3, 2, 1))

    shared = {
        "gam": cm(gamma), "bet": cm(beta),
        "winA": c(W_in.reshape(DC, 128, RA)),
        "b_in": c(b_in.reshape(RA, 1)),
        "dec_l": c(dec_l.reshape(RA, 1)),
        "woutA": c(np.concatenate([W_out, b_out[None, :]], 0)),
        "dskip": cm(D_skip),
        "wdnA": c(W_down.reshape(DC, 128, RF)),
        "b_dn": c(b_down.reshape(RF, 1)),
        "wupA": c(np.concatenate([W_up, b_up[None, :]], 0)),
        "xpwT": c(xpw.T.reshape(DC, 128, 160)),
        "dtwT": c(dtw.T),
        "dtbT": cm(dtb),
        "alog": c(A_log),
        "dparm": cm(D_param),
        "gwT": c(gate_w.T.reshape(DC, 128, E)),
        "w1t": w1t, "w2t": w2t, "sw1t": sw1t, "sw2t": sw2t,
    }
    nonce = int(np.random.randint(1, 2**30))
    in_maps = []
    for cc_ in range(NC):
        m = dict(shared)
        m["xT"] = c(xt[:, TLOC * cc_:TLOC * (cc_ + 1)])
        m["meta"] = np.array([[cc_, cc_ % 2, cc_ // 2, nonce, 0, 0, 0, 0]], np.int32)
        m["pmask8"] = np.full((RA, 1), float(cc_ % 2), np.float32)
        in_maps.append(m)
    return in_maps


def kernel(**inputs):
    in_maps = _prep(inputs)
    nc = build(upto="full")
    res = run_bass_kernel_spmd(nc, in_maps, core_ids=list(range(NC)))
    outs = [np.asarray(res.results[c]["outT"]) for c in range(NC)]
    full = np.concatenate(outs, axis=1)
    return np.ascontiguousarray(full.T).reshape(B, L, D).astype(np.float32)


# version 03 (final)
# speedup vs baseline: 1.0500x; 1.0500x over previous
import numpy as np
from contextlib import ExitStack

import concourse.bass as bass
import concourse.mybir as mybir
from concourse.bass_utils import run_bass_kernel_spmd

F32 = mybir.dt.float32
BF16 = mybir.dt.bfloat16
I32 = mybir.dt.int32
AF = mybir.ActivationFunctionType
OP = mybir.AluOpType
AX = mybir.AxisListType

B, L, D = 4, 8192, 2048
NC = 8
TLOC = 4096
DC = D // 128
EPS = 1e-5
RA, RF = 8, 32
E, I1, IS, TOPK = 16, 1408, 2816, 6
NJ, NJS = I1 // 128, IS // 128
TG = 1024
NG = TLOC // TG
NT = TG // 512
RDT, NS = 128, 16
TCH = 512
NCH = L // TCH
DTI = 8
MAGIC = 0x51AB0000
SET_LNEXP = 6
SET_SILU = 18
FLAG_CARRY, FLAG_XN, FLAG_Y = 0, 1, 2


class Sem:
    def __init__(self, es, nc, name):
        self.h = es.enter_context(nc.semaphore(name))
        self.n = 0

    def inc(self, instr, k=1):
        instr.then_inc(self.h, k)
        self.n += k
        return self.n


def load_act_set(nc, idx):
    return nc.scalar.add_instruction(mybir.InstLoadActFuncSet(
        name=nc.get_next_instruction_name(), act_func_set_id=idx, ins=[], outs=[]))


def fm(dram_ap, c0, ncols):
    return dram_ap[:, c0:c0 + ncols].rearrange("(k p) t -> p k t", p=128)


def build(upto="full", dbg=False):
    nc = bass.Bass("TRN2", target_bir_lowering=False, detect_race_conditions=False)
    es = ExitStack()
    EE = es.enter_context

    def din(name, shape, dtype=F32):
        return nc.dram_tensor(name, list(shape), dtype, kind="ExternalInput").ap()

    xT = din("xT", (D, TLOC))
    meta = din("meta", (1, 8), I32)
    pmask8 = din("pmask8", (RA, 1))
    gam = din("gam", (128, DC))
    bet = din("bet", (128, DC))
    winA = din("winA", (DC, 128, RA))
    b_in = din("b_in", (RA, 1))
    dec_l = din("dec_l", (RA, 1))
    woutA = din("woutA", (RA + 1, D))
    dskip = din("dskip", (128, DC))
    wdnA = din("wdnA", (DC, 128, RF))
    b_dn = din("b_dn", (RF, 1))
    wupA = din("wupA", (RF + 1, D))
    xpwT = din("xpwT", (DC, 128, 160))
    dtwT = din("dtwT", (RDT, D))
    dtbT = din("dtbT", (128, DC))
    alog = din("alog", (D, NS))
    dparm = din("dparm", (128, DC))
    gwT = din("gwT", (DC, 128, E))
    w1t = din("w1t", (E, NJ, 128, DC, 256))
    w2t = din("w2t", (E, DC, 128, NJ, 128))
    sw1t = din("sw1t", (NJS, 128, DC, 256))
    sw2t = din("sw2t", (DC, 128, NJS, 128))

    outT = nc.dram_tensor("outT", [D, TLOC], F32, kind="ExternalOutput").ap()

    x1d = nc.dram_tensor("x1d", [D, TLOC], F32).ap()
    x2d = nc.dram_tensor("x2d", [D, TLOC], F32).ap()
    x3d = nc.dram_tensor("x3d", [D, TLOC], F32).ap()
    xn3d = nc.dram_tensor("xn3d", [D, TLOC], F32).ap()
    stg1 = nc.dram_tensor("stg1", [D, TLOC], F32).ap()
    stg2 = nc.dram_tensor("stg2", [D, TLOC], F32).ap()
    w1b = nc.dram_tensor("w1b", [E, NJ, 128, DC, 256], BF16).ap()
    w2b = nc.dram_tensor("w2b", [E, DC, 128, NJ, 128], BF16).ap()
    sw1b = nc.dram_tensor("sw1b", [NJS, 128, DC, 256], BF16).ap()
    cmbd = nc.dram_tensor("cmbd", [E, TLOC], F32).ap()
    xp_d = nc.dram_tensor("xp_d", [160, TLOC], F32).ap()
    xloc = nc.dram_tensor("xloc", [1024, L], F32).ap()
    yloc1 = nc.dram_tensor("yloc1", [1024, L], F32).ap()
    yloc2 = nc.dram_tensor("yloc2", [2048, TLOC], F32).ap()
    sw2b = nc.dram_tensor("sw2b", [DC, 128, NJS, 128], BF16).ap()

    shm_xn = nc.dram_tensor("shm_xn", [2, D, TLOC], F32, addr_space="Shared").ap()
    shm_xp = nc.dram_tensor("shm_xp", [2, 160, TLOC], F32, addr_space="Shared").ap()
    shm_y = nc.dram_tensor("shm_y", [2, 2, 1024, TLOC], F32, addr_space="Shared").ap()
    shm_car = nc.dram_tensor("shm_car", [2, RA], F32, addr_space="Shared").ap()
    shm_fl = nc.dram_tensor("shm_fl", [2, 8], I32, addr_space="Shared").ap()

    sGs = Sem(es, nc, "sGs")
    sGp = Sem(es, nc, "sGp")
    sGc = Sem(es, nc, "sGc")
    sGx = Sem(es, nc, "sGx")
    sGw = Sem(es, nc, "sGw")
    sQw = [Sem(es, nc, f"sQw{i}") for i in range(2)]
    sQv = [Sem(es, nc, f"sQv{i}") for i in range(2)]
    sQx = [Sem(es, nc, f"sQx{i}") for i in range(2)]
    sS = Sem(es, nc, "sS")
    sK = Sem(es, nc, "sK")
    sP = Sem(es, nc, "sP")
    sA = Sem(es, nc, "sA")
    sV = Sem(es, nc, "sV")
    sC = Sem(es, nc, "sC")

    def relay():
        SY.wait_ge(sS.h, sS.n)
        SY.sem_inc(sK.h, 1)
        sK.n += 1
        return sK.n

    SY, GP, SC, VE, TE = nc.sync, nc.gpsimd, nc.scalar, nc.vector, nc.tensor

    r_par_sy = EE(SY.register("par_sy"))
    r_par_gp = EE(GP.register("par_gp"))
    r_fl = EE(GP.register("flread"))
    r_non = EE(GP.register("nonce"))
    r_ftmp = EE(GP.register("ftmp"))
    SY.reg_load(r_par_sy, meta[0:1, 1:2])
    GP.reg_load(r_par_gp, meta[0:1, 1:2])
    GP.reg_load(r_non, meta[0:1, 3:4])
    pv_sy = SY.snap(r_par_sy, min_val=0, max_val=1)
    pv_gp = GP.snap(r_par_gp, min_val=0, max_val=1)
    nonce_sv = GP.snap(r_non, min_val=0, max_val=2**30)
    gp_xn_base = GP.snap(pv_gp * D, min_val=0, max_val=2 * D)
    gp_xp_lo = GP.snap(pv_gp * 160, min_val=0, max_val=320)
    gp_xp_hi = GP.snap(pv_gp * 160 + 128, min_val=0, max_val=320)
    gp_1024 = GP.snap(pv_gp * 1024, min_val=0, max_val=2048)
    sy_1024 = SY.snap(pv_sy * 1024, min_val=0, max_val=2048)
    sy_2048 = SY.snap(pv_sy * 2048, min_val=0, max_val=4096)

    prm = ExitStack()
    PE_ = prm.enter_context
    sb_gam = EE(nc.sbuf_tensor("sb_gam", [128, DC], F32))
    sb_bet = EE(nc.sbuf_tensor("sb_bet", [128, DC], F32))
    sb_dsk = EE(nc.sbuf_tensor("sb_dsk", [128, DC], F32))
    sb_dpm = EE(nc.sbuf_tensor("sb_dpm", [128, DC], F32))
    sb_dtb = EE(nc.sbuf_tensor("sb_dtb", [128, DTI], F32))
    sb_xpw = EE(nc.sbuf_tensor("sb_xpw", [128, DC, 160], BF16))
    sb_gw = EE(nc.sbuf_tensor("sb_gw", [128, DC, E], BF16))
    sb_1b = EE(nc.sbuf_tensor("sb_1b", [128, 1], BF16))
    sb_1f = EE(nc.sbuf_tensor("sb_1f", [1, 128], F32))
    sb_idn = EE(nc.sbuf_tensor("sb_idn", [128, 128], F32))
    sb_flv = EE(nc.sbuf_tensor("sb_flv", [1, 8], I32))
    sb_acl = EE(nc.sbuf_tensor("sb_acl", [128, DTI, NS], F32))
    sb_dtw = EE(nc.sbuf_tensor("sb_dtw", [128, DTI, 128], BF16))
    sb_low = EE(nc.sbuf_tensor("sb_low", [128, E, E], BF16))
    sb_eps = EE(nc.sbuf_tensor("sb_eps", [128, 1], F32))
    sb_win = PE_(nc.sbuf_tensor("sb_win", [128, DC, RA], F32))
    sb_wout = PE_(nc.sbuf_tensor("sb_wout", [RA + 1, D], F32))
    sb_wdn = PE_(nc.sbuf_tensor("sb_wdn", [128, DC, RF], BF16))
    sb_wup = PE_(nc.sbuf_tensor("sb_wup", [RF + 1, D], BF16))
    sb_bin = PE_(nc.sbuf_tensor("sb_bin", [RA, 1], F32))
    sb_bdn = PE_(nc.sbuf_tensor("sb_bdn", [RF, 1], F32))
    sb_dec = PE_(nc.sbuf_tensor("sb_dec", [RA, 1], F32))
    sb_pm8 = PE_(nc.sbuf_tensor("sb_pm8", [RA, 1], F32))
    sb_z8 = PE_(nc.sbuf_tensor("sb_z8", [RA, 1], F32))

    pb = [EE(nc.psum_tensor(f"pb{i}", [128, 512], F32)) for i in range(8)]

    for ap_d, ap_s in [(sb_gam, gam), (sb_bet, bet), (sb_dsk, dskip),
                       (sb_dpm, dparm)]:
        sGs.inc(GP.dma_start(out=ap_d[:, :], in_=ap_s), 16)
    sGs.inc(GP.dma_start(out=sb_bin[:, :], in_=b_in), 16)
    sGs.inc(GP.dma_start(out=sb_bdn[:, :], in_=b_dn), 16)
    sGs.inc(GP.dma_start(out=sb_pm8[:, :], in_=pmask8), 16)
    gp_8 = GP.snap(pv_gp * DTI, min_val=0, max_val=16)
    sGs.inc(GP.dma_start(out=sb_dtb[:, :], in_=dtbT[:, bass.ds(gp_8, DTI)]), 16)
    sGs.inc(GP.dma_start(out=sb_win[:, :, :], in_=winA.transpose([1, 0, 2])), 16)
    sGs.inc(GP.dma_start(out=sb_wout[:, :], in_=woutA), 16)
    sGs.inc(GP.dma_start(out=sb_wdn[:, :, :], in_=wdnA.transpose([1, 0, 2])), 16)
    sGs.inc(GP.dma_start(out=sb_wup[:, :], in_=wupA), 16)
    sGs.inc(GP.dma_start(out=sb_xpw[:, :, :], in_=xpwT.transpose([1, 0, 2])), 16)
    sGs.inc(GP.dma_start(out=sb_gw[:, :, :], in_=gwT.transpose([1, 0, 2])), 16)
    sGs.inc(GP.dma_start(out=sb_dec[:, :], in_=dec_l), 16)
    sGs.inc(GP.dma_start(
        out=sb_dtw[:, :, :],
        in_=dtwT[:, bass.ds(gp_1024, 1024)].rearrange("r (i c) -> r i c", c=128)), 16)
    sGs.inc(GP.dma_start(
        out=sb_acl[:, :, :],
        in_=alog[bass.ds(gp_1024, 1024), :].rearrange("(i p) n -> p i n", p=128)), 16)
    SETUP_TOTAL = sGs.n

    for e in range(E):
        sGp.inc(GP.dma_start(out=w1b[e], in_=w1t[e]), 16)
        sGp.inc(GP.dma_start(out=w2b[e], in_=w2t[e]), 16)
    sGp.inc(GP.dma_start(out=sw1b, in_=sw1t), 16)
    sGp.inc(GP.dma_start(out=sw2b, in_=sw2t), 16)
    PRECAST_TOTAL = sGp.n

    GP.memset(sb_idn[:, :], 0.0)
    GP.affine_select(out=sb_idn[:, :], in_=sb_idn[:, :], compare_op=OP.not_equal,
                     fill=1.0, base=0, pattern=[[-1, 128]], channel_multiplier=1)
    GP.memset(sb_low[:, :, :], 0.0)
    GP.affine_select(out=sb_low[:, :, :], in_=sb_low[:, :, :], compare_op=OP.is_ge,
                     fill=1.0, base=0, pattern=[[-1, E], [1, E]], channel_multiplier=0)
    for s in range(8):
        GP.memset(sb_flv[:, s:s + 1], MAGIC + s)
    GP.memset(sb_eps[:, :], EPS)
    nc.const_aps.register(F32, EPS, sb_eps[:, 0:1])
    c_setup = sC.inc(GP.tensor_copy(sb_flv[:, 0:1], sb_flv[:, 0:1]))
    SC.wait_ge(sC.h, c_setup)

    VE.memset(sb_1b[:, :], 1.0)
    VE.memset(sb_1f[:, :], 1.0)
    VE.memset(sb_z8[:, :], 0.0)
    v_ones = sV.inc(VE.tensor_copy(sb_1f[:, 0:1], sb_1f[:, 0:1]))

    load_act_set(nc, SET_LNEXP)
    SC.wait_ge(sGs.h, SETUP_TOTAL)
    SC.activation(sb_dec[:, :], sb_dec[:, :], AF.Exp, scale=-1.0)
    a_dec = sA.inc(SC.activation(sb_dec[:, :], sb_dec[:, :], AF.Copy, bias=1.0))
    VE.wait_ge(sA.h, a_dec)
    v_dec = sV.inc(VE.reciprocal(sb_dec[:, :], sb_dec[:, :]))
    a_acl = sA.inc(SC.activation(sb_acl[:, :, :], sb_acl[:, :, :], AF.Exp))
    VE.wait_ge(sA.h, a_acl)
    v_acl = sV.inc(VE.tensor_scalar_mul(sb_acl[:, :, :], sb_acl[:, :, :], -1.0))

    def flag_write(slot):
        GP.reg_add(r_ftmp, r_non, slot)
        with GP.If_eq(r_par_gp, 0):
            GP.reg_save(shm_fl[0:1, slot:slot + 1], r_ftmp)
        with GP.Else():
            GP.reg_save(shm_fl[1:2, slot:slot + 1], r_ftmp)

    def flag_poll(slot, rows=(0, 1)):
        for r in rows:
            GP.reg_mov(r_fl, 0)

            def cond():
                GP.reg_load(r_fl, shm_fl[r:r + 1, slot:slot + 1])
                return GP.snap(r_fl) != (nonce_sv + slot)
            with GP.While(cond):
                GP.engine_nop()
        return sC.inc(GP.tensor_copy(sb_flv[:, slot:slot + 1], sb_flv[:, slot:slot + 1]))

    def emit_ln_stats(bf_chunk, st_row, ps_sum, ps_sq, ps_mu, ps_r, sq2):
        for k in range(DC):
            TE.matmul(ps_sum[0:1, 0:512], sb_1b[:, :], bf_chunk(k),
                      start=(k == 0), stop=(k == DC - 1))
        pmk = {}
        for k in range(DC):
            if k >= 2:
                SC.wait_ge(sP.h, pmk[k - 2])
            a_sq = sA.inc(SC.activation(sq2[:, k % 2, :], bf_chunk(k), AF.Square))
            TE.wait_ge(sA.h, a_sq)
            pmk[k] = sP.inc(TE.matmul(ps_sq[0:1, 0:512], sb_1b[:, :], sq2[:, k % 2, :],
                                      start=(k == 0), stop=(k == DC - 1)))
        VE.wait_ge(sP.h, pmk[DC - 1])
        VE.tensor_scalar_mul(st_row[:, 0, :], ps_sum[0:1, 0:512], -1.0 / D)
        VE.tensor_mul(st_row[:, 1, :], st_row[:, 0, :], st_row[:, 0, :])
        v_var = sV.inc(VE.scalar_tensor_tensor(st_row[:, 1, :], ps_sq[0:1, 0:512],
                                               1.0 / D, st_row[:, 1, :],
                                               OP.mult, OP.subtract))
        SC.wait_ge(sV.h, v_var)
        SC.activation(st_row[:, 1, :], st_row[:, 1, :], AF.Ln, bias=EPS)
        a_r = sA.inc(SC.activation(st_row[:, 1, :], st_row[:, 1, :], AF.Exp, scale=-0.5))
        TE.wait_ge(sA.h, a_r)
        TE.matmul(ps_mu[:, 0:512], sb_1f[:, :], st_row[:, 0, :], start=True, stop=True)
        return sP.inc(TE.matmul(ps_r[:, 0:512], sb_1f[:, :], st_row[:, 1, :],
                                start=True, stop=True))

    pa = ExitStack()
    A = pa.enter_context
    bf_a = A(nc.sbuf_tensor("bf_a", [128, DC, 512], BF16))
    f_a = A(nc.sbuf_tensor("f_a", [128, DC, 512], F32))
    f_b = A(nc.sbuf_tensor("f_b", [128, DC, 512], F32))
    f_c = A(nc.sbuf_tensor("f_c", [128, DC, 512], F32))
    sq_a = A(nc.sbuf_tensor("sq_a", [128, 2, 512], BF16))
    st_a = A(nc.sbuf_tensor("st_a", [1, 2, 512], F32))
    u_sb = A(nc.sbuf_tensor("u_sb", [RA + 1, TLOC], F32))
    dr_a = A(nc.sbuf_tensor("dr_a", [RA, TLOC], F32))
    hc_a = A(nc.sbuf_tensor("hc_a", [RA, TLOC], F32))
    car8 = A(nc.sbuf_tensor("car8", [RA, 1], F32))
    hf_a = A(nc.sbuf_tensor("hf_a", [RF + 1, 512], BF16))

    TE.wait_ge(sV.h, v_ones)
    TE.wait_ge(sGs.h, SETUP_TOTAL)
    VE.memset(u_sb[:, :], 1.0)
    v_useed = sV.inc(VE.tensor_copy(u_sb[0:1, 0:1], u_sb[0:1, 0:1]))
    SC.wait_ge(sV.h, v_useed)
    a_u_last = 0
    for t8 in range(8):
        c0 = 512 * t8
        q_x = sQx[t8 % 2].inc(SY.dma_start(out=f_a[:, :, :], in_=fm(xT, c0, 512)), 16)
        q_x = (t8 % 2, q_x)
        GP.wait_ge(sP.h, sP.n)
        GP.wait_ge(sA.h, sA.n)
        g_x = sGx.inc(GP.dma_start(out=bf_a[:, :, :], in_=fm(xT, c0, 512)), 16)
        TE.wait_ge(sGx.h, g_x)
        SC.wait_ge(sGx.h, g_x)
        TE.wait_ge(sV.h, sV.n)
        p_bc = emit_ln_stats(lambda k: bf_a[:, k, :], st_a, pb[0], pb[1],
                             pb[2], pb[3], sq_a)
        VE.wait_ge(sP.h, p_bc)
        VE.wait_ge(sQx[q_x[0]].h, q_x[1])
        for k in range(DC):
            VE.tensor_add(f_b[:, k, :], f_a[:, k, :], pb[2][:, 0:512])
            VE.tensor_mul(f_b[:, k, :], f_b[:, k, :], pb[3][:, 0:512])
            VE.tensor_copy(bf_a[:, k, :], f_b[:, k, :])
            VE.tensor_add(f_a[:, k, :], f_a[:, k, :], f_b[:, k, :])
        v_z = sV.inc(VE.tensor_copy(st_a[:, 0, 0:1], st_a[:, 0, 0:1]))
        TE.wait_ge(sV.h, v_z)
        SC.wait_ge(sV.h, v_z)
        p_bc = emit_ln_stats(lambda k: bf_a[:, k, :], st_a, pb[0], pb[1],
                             pb[2], pb[3], sq_a)
        VE.wait_ge(sP.h, p_bc)
        for k in range(DC):
            VE.tensor_add(f_c[:, k, :], f_b[:, k, :], pb[2][:, 0:512])
            VE.tensor_mul(f_c[:, k, :], f_c[:, k, :], pb[3][:, 0:512])
            VE.tensor_scalar(f_c[:, k, :], f_c[:, k, :], sb_gam[:, k:k + 1],
                             sb_bet[:, k:k + 1], OP.mult, OP.add)
        v_xg = sV.inc(VE.tensor_copy(st_a[:, 0, 0:1], st_a[:, 0, 0:1]))
        TE.wait_ge(sV.h, v_xg)
        for k in range(DC):
            mm = TE.matmul(pb[4][0:RA, 0:512], sb_win[:, k, :], f_c[:, k, :],
                           start=(k == 0), stop=(k == DC - 1))
        p_u = sP.inc(mm)
        SC.wait_ge(sP.h, p_u)
        a_u_last = sA.inc(SC.activation(u_sb[0:RA, c0:c0 + 512], pb[4][0:RA, 0:512],
                                        AF.Identity, bias=sb_bin[:, 0:1]))
        SY.wait_ge(sV.h, v_xg)
        sS.inc(SY.dma_start(out=fm(stg1, c0, 512), in_=f_a[:, :, :]), 16)
        sS.inc(SY.dma_start(out=fm(stg2, c0, 512), in_=f_c[:, :, :]), 16)

    VE.wait_ge(sV.h, v_dec)
    VE.tensor_copy(dr_a[:, :], sb_dec[:, 0:1].broadcast_to([RA, TLOC]))
    VE.wait_ge(sA.h, a_u_last)
    if dbg:
        SY.wait_ge(sA.h, a_u_last)
        sS.inc(SY.dma_start(out=x3d[16:16 + RA, 0:TLOC], in_=u_sb[0:RA, :]), 16)
        SY.wait_ge(sS.h, sS.n)
        VE.wait_ge(sS.h, sS.n)
    VE.tensor_tensor_scan(u_sb[0:RA, :], dr_a[:, :], u_sb[0:RA, :],
                          0.0, OP.mult, OP.add)
    v_scan = sV.inc(VE.tensor_copy(st_a[:, 0, 0:1], st_a[:, 0, 0:1]))
    if dbg:
        SY.wait_ge(sV.h, v_scan)
        sS.inc(SY.dma_start(out=x3d[32:32 + RA, 0:TLOC], in_=dr_a[:, :]), 16)
        sS.inc(SY.dma_start(out=x3d[48:48 + RA, 0:TLOC], in_=u_sb[0:RA, :]), 16)
        SY.wait_ge(sS.h, sS.n)
    GP.wait_ge(sV.h, v_scan)
    GP.wait_ge(sC.h, c_setup)
    with GP.If_eq(r_par_gp, 0):
        sGc.n += 16
        GP.dma_start(out=shm_car[0:1, :].transpose([1, 0]),
                     in_=u_sb[0:RA, TLOC - 1:TLOC]).then_inc(sGc.h, 16)
    with GP.Else():
        GP.dma_start(out=shm_car[1:2, :].transpose([1, 0]),
                     in_=u_sb[0:RA, TLOC - 1:TLOC]).then_inc(sGc.h, 16)
    GP.wait_ge(sGc.h, sGc.n)
    flag_write(FLAG_CARRY)
    c_car = flag_poll(FLAG_CARRY, rows=(0,))
    g_car = sGc.inc(GP.dma_start(out=car8[:, :], in_=shm_car[0:1, :].transpose([1, 0])), 16)
    GP.wait_ge(sGc.h, g_car)
    c_mul = sC.inc(GP.tensor_tensor(car8[:, :], car8[:, :], sb_pm8[:, :], OP.mult))
    VE.wait_ge(sC.h, c_mul)
    VE.tensor_tensor_scan(hc_a[:, :], dr_a[:, :],
                          sb_z8[:, 0:1].broadcast_to([RA, TLOC]), car8[:, 0:1],
                          OP.mult, OP.add)
    VE.tensor_add(u_sb[0:RA, :], u_sb[0:RA, :], hc_a[:, :])
    v_h = sV.inc(VE.tensor_copy(st_a[:, 0, 0:1], st_a[:, 0, 0:1]))
    SY.wait_ge(sV.h, v_h)
    sS.inc(SY.dma_start(out=x3d[0:RA + 1, 0:TLOC], in_=u_sb[:, :]), 16)

    for t8 in range(8):
        c0 = 512 * t8
        sQx[t8 % 2].inc(SY.dma_start(out=f_a[:, :, :], in_=fm(stg1, c0, 512)), 16)
        q_xg = sQx[t8 % 2].inc(SY.dma_start(out=f_c[:, :, :], in_=fm(stg2, c0, 512)), 16)
        TE.wait_ge(sV.h, v_h)
        TE.wait_ge(sV.h, sV.n)
        VE.wait_ge(sQx[t8 % 2].h, q_xg)
        vmk = {}
        for k in range(DC):
            bank = pb[4 + (k % 2)]
            if k >= 2:
                TE.wait_ge(sV.h, vmk[k - 2])
            p_ar = sP.inc(TE.matmul(bank[:, 0:512], sb_wout[:, 128 * k:128 * (k + 1)],
                                    u_sb[:, c0:c0 + 512], start=True, stop=True))
            VE.wait_ge(sP.h, p_ar)
            VE.scalar_tensor_tensor(f_b[:, k, :], f_c[:, k, :], sb_dsk[:, k:k + 1],
                                    bank[:, 0:512], OP.mult, OP.add)
            vmk[k] = sV.inc(VE.tensor_add(f_b[:, k, :], f_b[:, k, :], f_a[:, k, :]))
        for k in range(DC):
            vm = VE.tensor_copy(bf_a[:, k, :], f_b[:, k, :])
        v_z1b = sV.inc(vm)
        TE.wait_ge(sV.h, v_z1b)
        SC.wait_ge(sV.h, v_z1b)
        p_bc = emit_ln_stats(lambda k: bf_a[:, k, :], st_a, pb[0], pb[1],
                             pb[2], pb[3], sq_a)
        VE.wait_ge(sP.h, p_bc)
        for k in range(DC):
            VE.tensor_add(f_c[:, k, :], f_b[:, k, :], pb[2][:, 0:512])
            VE.tensor_mul(f_c[:, k, :], f_c[:, k, :], pb[3][:, 0:512])
            vm = VE.tensor_scalar(bf_a[:, k, :], f_c[:, k, :], sb_gam[:, k:k + 1],
                                  sb_bet[:, k:k + 1], OP.mult, OP.add)
        v_xn2 = sV.inc(vm)
        TE.wait_ge(sV.h, v_xn2)
        for k in range(DC):
            mm = TE.matmul(pb[6][0:RF, 0:512], sb_wdn[:, k, :], bf_a[:, k, :],
                           start=(k == 0), stop=(k == DC - 1))
        p_hf = sP.inc(mm)
        SC.wait_ge(sP.h, p_hf)
        a_hf = sA.inc(SC.activation(hf_a[0:RF, :], pb[6][0:RF, 0:512],
                                    AF.Relu, bias=sb_bdn[:, 0:1]))
        if t8 == 0:
            v_o = sV.inc(VE.tensor_copy(hf_a[RF:RF + 1, :],
                                        sb_1b[0:1, 0:1].broadcast_to([1, 512])))
            TE.wait_ge(sV.h, v_o)
        TE.wait_ge(sA.h, a_hf)
        vmk = {}
        for k in range(DC):
            bank = pb[4 + (k % 2)]
            if k >= 2:
                TE.wait_ge(sV.h, vmk[k - 2])
            p_xf = sP.inc(TE.matmul(bank[:, 0:512], sb_wup[:, 128 * k:128 * (k + 1)],
                                    hf_a[:, :], start=True, stop=True))
            VE.wait_ge(sP.h, p_xf)
            vmk[k] = sV.inc(VE.tensor_add(f_b[:, k, :], f_b[:, k, :], bank[:, 0:512]))
        v_x1 = vmk[DC - 1]
        SY.wait_ge(sV.h, v_x1)
        sS.inc(SY.dma_start(out=fm(x1d, c0, 512), in_=f_b[:, :, :]), 16)
    pa.close()
    prm.close()

    def copy_out(src_d):
        SY.wait_ge(sS.h, sS.n)
        with nc.sbuf_tensor("cpout", [128, DC, 512], F32) as cp:
            for t8 in range(8):
                c0 = 512 * t8
                q1 = sQx[0].inc(SY.dma_start(out=cp[:, :, :], in_=fm(src_d, c0, 512)), 16)
                SY.wait_ge(sQx[0].h, q1)
                sS.inc(SY.dma_start(out=fm(outT, c0, 512), in_=cp[:, :, :]), 16)
                SY.wait_ge(sS.h, sS.n)
        GP.wait_ge(sK.h, relay())
        es.close()
        return nc

    if upto == "a":
        if dbg:
            dbg_u = nc.dram_tensor("dbg_u", [64, TLOC], F32, kind="ExternalOutput").ap()
            dbg_ac = nc.dram_tensor("dbg_ac", [D, TLOC], F32, kind="ExternalOutput").ap()
            dbg_xg = nc.dram_tensor("dbg_xg", [D, TLOC], F32, kind="ExternalOutput").ap()
            with nc.sbuf_tensor("dbg_cp", [128, DC, 512], F32) as dcp, \
                 nc.sbuf_tensor("dbg_uc", [64, TLOC], F32) as duc:
                SY.wait_ge(sV.h, sV.n)
                SY.wait_ge(sS.h, sS.n)
                for t8 in range(8):
                    c0 = 512 * t8
                    q1 = sQx[0].inc(SY.dma_start(out=dcp[:, :, :], in_=fm(stg1, c0, 512)), 16)
                    SY.wait_ge(sQx[0].h, q1)
                    sS.inc(SY.dma_start(out=fm(dbg_ac, c0, 512), in_=dcp[:, :, :]), 16)
                    SY.wait_ge(sS.h, sS.n)
                    q1 = sQx[0].inc(SY.dma_start(out=dcp[:, :, :], in_=fm(stg2, c0, 512)), 16)
                    SY.wait_ge(sQx[0].h, q1)
                    sS.inc(SY.dma_start(out=fm(dbg_xg, c0, 512), in_=dcp[:, :, :]), 16)
                    SY.wait_ge(sS.h, sS.n)
                sQx[0].inc(SY.dma_start(out=duc[:, :], in_=x3d[0:64, 0:TLOC]), 16)
                SY.wait_ge(sQx[0].h, sQx[0].n)
                sS.inc(SY.dma_start(out=dbg_u, in_=duc[:, :]), 16)
                SY.wait_ge(sS.h, sS.n)
        return copy_out(x1d)

    def emit_ffn(xin_d, xout_d, first_ffn):
        sfx = "A" if first_ffn else "B"
        pf = ExitStack()
        FA = pf.enter_context
        xn_bf = FA(nc.sbuf_tensor(f"xn_bf{sfx}", [128, DC, TG], BF16))
        y_sb = FA(nc.sbuf_tensor(f"y_sb{sfx}", [128, DC, TG], F32))
        act_bf = FA(nc.sbuf_tensor(f"act_bf{sfx}", [128, NJ, TG], BF16))
        w1sb = [FA(nc.sbuf_tensor(f"w1sb{i}{sfx}", [128, DC, 256], BF16)) for i in range(2)]
        w2sb = [FA(nc.sbuf_tensor(f"w2sb{i}{sfx}", [128, NJ, 128], BF16)) for i in range(2)]
        silu = FA(nc.sbuf_tensor(f"silu{sfx}", [128, 2, 2, 512], F32))
        xf32 = FA(nc.sbuf_tensor(f"xf32{sfx}", [128, 2, 512], F32))
        sqf = FA(nc.sbuf_tensor(f"sqf{sfx}", [128, 2, 512], BF16))
        strw = FA(nc.sbuf_tensor(f"strw{sfx}", [1, 2, 512], F32))
        lgt = FA(nc.sbuf_tensor(f"lgt{sfx}", [E, TG], F32))
        ltok = FA(nc.sbuf_tensor(f"ltok{sfx}", [128, 8, E], F32))
        cpa = FA(nc.sbuf_tensor(f"cpa{sfx}", [128, 8, E, E], BF16))
        cpb = FA(nc.sbuf_tensor(f"cpb{sfx}", [128, 8, E, E], BF16))
        cntr = FA(nc.sbuf_tensor(f"cntr{sfx}", [128, 8, E], F32))
        cntb = FA(nc.sbuf_tensor(f"cntb{sfx}", [128, 8, E], F32))
        ssm = FA(nc.sbuf_tensor(f"ssm{sfx}", [128, 8, 1], F32))
        ssb = FA(nc.sbuf_tensor(f"ssb{sfx}", [128, 8, 1], F32))
        rsb = FA(nc.sbuf_tensor(f"rsb{sfx}", [128, 8, 1], F32))
        mskb = FA(nc.sbuf_tensor(f"mskb{sfx}", [128, 8, E], F32))
        cmb = FA(nc.sbuf_tensor(f"cmb{sfx}", [128, 8, E], F32))
        cmbT = FA(nc.sbuf_tensor(f"cmbT{sfx}", [E, TG], F32))
        cbc = FA(nc.sbuf_tensor(f"cbc{sfx}", [128, 2, 512], F32))
        xnf = FA(nc.sbuf_tensor(f"xnf{sfx}", [128, 2, 512], F32))
        xpo = FA(nc.sbuf_tensor(f"xpo{sfx}", [128, 512], F32))
        xpo2 = FA(nc.sbuf_tensor(f"xpo2{sfx}", [32, 512], F32))

        if first_ffn:
            SY.wait_ge(sGp.h, PRECAST_TOTAL)

        for g in range(NG):
            t0 = TG * g
            TE.wait_ge(sV.h, sV.n)
            TE.wait_ge(sA.h, sA.n)
            GP.wait_ge(sP.h, sP.n)
            GP.wait_ge(sV.h, sV.n)
            GP.wait_ge(sA.h, sA.n)
            GP.wait_ge(sK.h, relay())
            g_xb = sGx.inc(GP.dma_start(out=xn_bf[:, :, :], in_=fm(xin_d, t0, TG)), 16)
            load_act_set(nc, SET_LNEXP)
            xfmk = {}
            lidx = 0
            for tt in range(NT):
                cs = slice(512 * tt, 512 * (tt + 1))
                TE.wait_ge(sGx.h, g_xb)
                SC.wait_ge(sGx.h, g_xb)
                if tt > 0:
                    TE.wait_ge(sV.h, sV.n)
                p_bc = emit_ln_stats(lambda k: xn_bf[:, k, cs], strw,
                                     pb[4], pb[5], pb[4], pb[5], sqf)
                VE.wait_ge(sP.h, p_bc)
                for k in range(DC):
                    buf = lidx % 2
                    if lidx >= 2:
                        SY.wait_ge(sV.h, xfmk[lidx - 2])
                    q_xf = sQx[buf].inc(SY.dma_start(
                        out=xf32[:, buf, :],
                        in_=xin_d[128 * k:128 * (k + 1),
                                  t0 + 512 * tt:t0 + 512 * (tt + 1)]), 16)
                    VE.wait_ge(sQx[buf].h, q_xf)
                    VE.tensor_add(xf32[:, buf, :], xf32[:, buf, :], pb[4][:, 0:512])
                    xfmk[lidx] = sV.inc(
                        VE.tensor_mul(xn_bf[:, k, cs], xf32[:, buf, :], pb[5][:, 0:512]))
                    lidx += 1
                v_xn = xfmk[lidx - 1]
            TE.wait_ge(sV.h, v_xn)
            for tt in range(NT):
                cs = slice(512 * tt, 512 * (tt + 1))
                for k in range(DC):
                    mm = TE.matmul(pb[6][0:E, 0:512], sb_gw[:, k, :], xn_bf[:, k, cs],
                                   start=(k == 0), stop=(k == DC - 1))
                p_lg = sP.inc(mm)
                SC.wait_ge(sP.h, p_lg)
                a_lg = sA.inc(SC.activation(lgt[:, cs], pb[6][0:E, 0:512], AF.Exp))
                TE.wait_ge(sA.h, a_lg)
            for c8 in range(8):
                TE.wait_ge(sA.h, sA.n)
                p_tr = sP.inc(TE.transpose(pb[7][:, 0:E],
                                           lgt[0:E, 128 * c8:128 * (c8 + 1)],
                                           sb_idn[0:16, 0:16]))
                SC.wait_ge(sP.h, p_tr)
                sA.inc(SC.activation(ltok[:, c8, :], pb[7][:, 0:E], AF.Copy))
            VE.wait_ge(sA.h, sA.n)
            li = ltok[:, :, :].unsqueeze(3).broadcast_to([128, 8, E, E])
            lj = ltok[:, :, :].unsqueeze(2).broadcast_to([128, 8, E, E])
            VE.tensor_tensor(cpa[:, :, :, :], lj, li, OP.is_gt)
            VE.tensor_tensor(cpb[:, :, :, :], lj, li, OP.is_equal)
            VE.tensor_tensor(cpb[:, :, :, :], cpb[:, :, :, :],
                             sb_low[:, :, :].unsqueeze(1).broadcast_to([128, 8, E, E]),
                             OP.mult)
            VE.tensor_tensor(cpa[:, :, :, :], cpa[:, :, :, :], cpb[:, :, :, :], OP.add)
            VE.tensor_reduce(cntr[:, :, :].unsqueeze(3), cpa[:, :, :, :], AX.X, OP.add)
            v_ct = sV.inc(VE.tensor_reduce(ssm[:, :, :], ltok[:, :, :], AX.X, OP.add))
            GP.wait_ge(sV.h, v_ct)
            GP.tensor_copy(cntb[:, :, :], cntr[:, :, :])
            c_ct = sC.inc(GP.tensor_copy(ssb[:, :, :], ssm[:, :, :]))
            VE.wait_ge(sC.h, c_ct)
            VE.reciprocal(rsb[:, :, :], ssb[:, :, :])
            VE.tensor_single_scalar(mskb[:, :, :], cntb[:, :, :], float(TOPK) - 0.5,
                                    OP.is_lt)
            v_cu = sV.inc(VE.tensor_mul(cmb[:, :, :], ltok[:, :, :], mskb[:, :, :]))
            GP.wait_ge(sV.h, v_cu)
            c_cm = sC.inc(GP.tensor_tensor(cmb[:, :, :], cmb[:, :, :],
                                           rsb[:, :, :].broadcast_to([128, 8, E]),
                                           OP.mult))
            TE.wait_ge(sC.h, c_cm)
            for c8 in range(8):
                TE.wait_ge(sA.h, sA.n)
                p_tb = sP.inc(TE.transpose(pb[7][0:E, 0:128], cmb[:, c8, :],
                                           sb_idn[:, :]))
                SC.wait_ge(sP.h, p_tb)
                sA.inc(SC.activation(cmbT[:, 128 * c8:128 * (c8 + 1)],
                                     pb[7][0:E, 0:128], AF.Copy))
            GP.wait_ge(sA.h, sA.n)
            sGw.inc(GP.dma_start(out=cmbd[:, t0:t0 + TG], in_=cmbT[:, :]), 16)
            GP.wait_ge(sGw.h, sGw.n)
            load_act_set(nc, SET_SILU)
            iters = [("e", e, j) for e in range(E) for j in range(NJ)]
            iters += [("s", 0, j) for j in range(NJ)]
            iters += [("s", 1, j) for j in range(NJ)]
            it = 0
            st1 = {}
            prev_key = None
            cbc_mark = None
            p_y_prev_exp = 0
            st2_prev = (0, 0)
            v_act = 0
            for kind, e, j in iters:
                buf = it % 2
                njc = NJ
                if (kind, e) != prev_key:
                    if (dbg and first_ffn and g == 0 and kind == "s" and e == 0):
                        SY.wait_ge(sV.h, sV.n)
                        SY.wait_ge(sA.h, sA.n)
                        sS.inc(SY.dma_start(out=fm(x3d, 0, TG), in_=y_sb[:, :, :]), 16)
                        SY.wait_ge(sS.h, sS.n)
                    prev_key = (kind, e)
                    first_of_exp = it
                    if kind == "e":
                        GP.wait_ge(sV.h, sV.n)
                        cbc_mark = sGx.inc(GP.dma_start(
                            out=cbc[:, :, :].rearrange("p a t -> p (a t)"),
                            in_=cmbd[e:e + 1, t0:t0 + TG].broadcast_to([128, TG])), 16)
                if it >= 2:
                    SY.wait_ge(sP.h, st1[it - 2][0][NT - 1])
                wsrc = w1b[e, j] if kind == "e" else sw1b[NJ * e + j]
                q_w = sQw[buf].inc(SY.dma_start(out=w1sb[buf][:, :, :],
                                                in_=wsrc), 16)
                TE.wait_ge(sQw[buf].h, q_w)
                pgu_l, asi_l, vup_l, vfi_l = [], [], [], []
                for tt in range(NT):
                    cs = slice(512 * tt, 512 * (tt + 1))
                    gps, ups = pb[0 + tt], pb[2 + tt]
                    if it >= 1:
                        TE.wait_ge(sA.h, st1[it - 1][1][tt])
                    for k in range(DC):
                        TE.matmul(gps[:, 0:512], w1sb[buf][:, k, 0:128], xn_bf[:, k, cs],
                                  start=(k == 0), stop=(k == DC - 1))
                    if it >= 1:
                        TE.wait_ge(sV.h, st1[it - 1][2][tt])
                    for k in range(DC):
                        mm = TE.matmul(ups[:, 0:512], w1sb[buf][:, k, 128:256],
                                       xn_bf[:, k, cs], start=(k == 0), stop=(k == DC - 1))
                    p_gu = sP.inc(mm)
                    SC.wait_ge(sP.h, p_gu)
                    if it >= 2:
                        SC.wait_ge(sV.h, st1[it - 2][3][tt])
                    a_si = sA.inc(SC.activation(silu[:, buf, tt, :], gps[:, 0:512],
                                                AF.Silu))
                    VE.wait_ge(sP.h, p_gu)
                    VE.wait_ge(sA.h, a_si)
                    if it == first_of_exp and tt == 0:
                        VE.wait_ge(sP.h, p_y_prev_exp)
                        if kind == "e":
                            VE.wait_ge(sGx.h, cbc_mark)
                    if kind == "e":
                        v_up = sV.inc(VE.tensor_mul(silu[:, buf, tt, :],
                                                    silu[:, buf, tt, :], ups[:, 0:512]))
                        v_fi = sV.inc(VE.tensor_mul(act_bf[:, j, cs], silu[:, buf, tt, :],
                                                    cbc[:, tt, :]))
                    else:
                        v_fi = sV.inc(VE.tensor_mul(act_bf[:, j, cs], silu[:, buf, tt, :],
                                                    ups[:, 0:512]))
                        v_up = v_fi
                    pgu_l.append(p_gu)
                    asi_l.append(a_si)
                    vup_l.append(v_up)
                    vfi_l.append(v_fi)
                st1[it] = (pgu_l, asi_l, vup_l, vfi_l)
                v_act = vfi_l[NT - 1]
                it += 1
                if j == njc - 1:
                    first_e = (kind == "e" and e == 0)
                    if kind == "e":
                        w2src = (lambda ee: lambda hc: w2b[ee, hc])(e)
                    else:
                        w2src = (lambda hh: lambda hc: sw2b[hc][:, NJ * hh:NJ * (hh + 1), :])(e)
                    W2Q = SY if first_e else SC
                    st2 = {}
                    for hc in range(DC):
                        b2 = hc % 2
                        if hc >= 2:
                            W2Q.wait_ge(sP.h, st2[hc - 2][0])
                        q_w2 = sQv[b2].inc(W2Q.dma_start(out=w2sb[b2][:, 0:njc, :],
                                                         in_=w2src(hc)), 16)
                        TE.wait_ge(sQv[b2].h, q_w2)
                        if hc == 0:
                            TE.wait_ge(sV.h, v_act)
                            TE.wait_ge(sV.h, st2_prev[0])
                            TE.wait_ge(sA.h, st2_prev[1])
                        vy_l, ay_l = [], []
                        for tt in range(NT):
                            cs = slice(512 * tt, 512 * (tt + 1))
                            yps = pb[4 + 2 * (hc % 2) + tt]
                            if hc >= 2:
                                TE.wait_ge(sV.h, st2[hc - 2][1][tt])
                                TE.wait_ge(sA.h, st2[hc - 2][2][tt])
                            for ic in range(njc):
                                mm = TE.matmul(yps[:, 0:512], w2sb[b2][:, ic, :],
                                               act_bf[:, ic, cs],
                                               start=(ic == 0), stop=(ic == njc - 1))
                            p_y = sP.inc(mm)
                            if first_e:
                                SC.wait_ge(sP.h, p_y)
                                a_y = sA.inc(SC.activation(y_sb[:, hc, cs],
                                                           yps[:, 0:512], AF.Copy))
                                v_y = sV.n
                            else:
                                VE.wait_ge(sP.h, p_y)
                                if kind == "e" and e == 1 and hc == 0 and tt == 0:
                                    VE.wait_ge(sA.h, sA.n)
                                v_y = sV.inc(VE.tensor_add(y_sb[:, hc, cs],
                                                           y_sb[:, hc, cs], yps[:, 0:512]))
                                a_y = sA.n
                            vy_l.append(v_y)
                            ay_l.append(a_y)
                        st2[hc] = (p_y, vy_l, ay_l)
                    p_y_prev_exp = st2[DC - 1][0]
                    st2_prev = (st2[DC - 1][1][NT - 1], st2[DC - 1][2][NT - 1])
            v_yfin, a_yfin = sV.n, sA.n
            load_act_set(nc, SET_LNEXP)
            VE.wait_ge(sV.h, v_yfin)
            VE.wait_ge(sA.h, a_yfin)
            fmk = {}
            idx = 0
            for k in range(DC):
                for tt in range(NT):
                    buf = idx % 2
                    if idx >= 2:
                        SY.wait_ge(sV.h, fmk[idx - 2])
                    q_xf = sQx[buf].inc(SY.dma_start(
                        out=xf32[:, buf, :],
                        in_=xin_d[128 * k:128 * (k + 1),
                                  t0 + 512 * tt:t0 + 512 * (tt + 1)]), 16)
                    VE.wait_ge(sQx[buf].h, q_xf)
                    cs = slice(512 * tt, 512 * (tt + 1))
                    fmk[idx] = sV.inc(VE.tensor_add(xf32[:, buf, :], xf32[:, buf, :],
                                                    y_sb[:, k, cs]))
                    SY.wait_ge(sV.h, fmk[idx])
                    sS.inc(SY.dma_start(
                        out=xout_d[128 * k:128 * (k + 1),
                                   t0 + 512 * tt:t0 + 512 * (tt + 1)],
                        in_=xf32[:, buf, :]), 16)
                    idx += 1
            q_xout = sS.n
            if first_ffn:
                GP.wait_ge(sP.h, sP.n)
                GP.wait_ge(sV.h, sV.n)
                GP.wait_ge(sA.h, sA.n)
                GP.wait_ge(sK.h, relay())
                g_x2 = sGx.inc(GP.dma_start(out=xn_bf[:, :, :],
                                            in_=fm(xout_d, t0, TG)), 16)
                for tt in range(NT):
                    cs = slice(512 * tt, 512 * (tt + 1))
                    TE.wait_ge(sGx.h, g_x2)
                    SC.wait_ge(sGx.h, g_x2)
                    TE.wait_ge(sV.h, sV.n)
                    TE.wait_ge(sA.h, sA.n)
                    p_bc = emit_ln_stats(lambda k: xn_bf[:, k, cs], strw,
                                         pb[4], pb[5], pb[4], pb[5], sqf)
                    VE.wait_ge(sP.h, p_bc)
                    nmk = {}
                    for k in range(DC):
                        buf = (tt * DC + k) % 2
                        q_xf = sQx[buf].inc(SY.dma_start(
                            out=xf32[:, buf, :],
                            in_=xout_d[128 * k:128 * (k + 1),
                                       t0 + 512 * tt:t0 + 512 * (tt + 1)]), 16)
                        VE.wait_ge(sQx[buf].h, q_xf)
                        VE.tensor_add(xnf[:, buf, :], xf32[:, buf, :], pb[4][:, 0:512])
                        VE.tensor_mul(xnf[:, buf, :], xnf[:, buf, :], pb[5][:, 0:512])
                        nmk[k] = sV.inc(VE.tensor_copy(xn_bf[:, k, cs], xnf[:, buf, :]))
                        SY.wait_ge(sV.h, nmk[k])
                        sS.inc(SY.dma_start(
                            out=xn3d[128 * k:128 * (k + 1),
                                     t0 + 512 * tt:t0 + 512 * (tt + 1)],
                            in_=xnf[:, buf, :]), 16)

                    v_xn3 = nmk[DC - 1]
                    TE.wait_ge(sV.h, v_xn3)
                    TE.wait_ge(sA.h, sA.n)
                    for k in range(DC):
                        TE.matmul(pb[6][0:128, 0:512], sb_xpw[:, k, 0:128],
                                  xn_bf[:, k, cs], start=(k == 0), stop=(k == DC - 1))
                    for k in range(DC):
                        mm = TE.matmul(pb[7][0:32, 0:512], sb_xpw[:, k, 128:160],
                                       xn_bf[:, k, cs], start=(k == 0), stop=(k == DC - 1))
                    p_xp = sP.inc(mm)
                    SC.wait_ge(sP.h, p_xp)
                    SC.activation(xpo[:, :], pb[6][0:128, 0:512], AF.Copy)
                    a_xp = sA.inc(SC.activation(xpo2[:, :], pb[7][0:32, 0:512],
                                                AF.Copy))
                    GP.wait_ge(sA.h, a_xp)
                    sGw.inc(GP.dma_start(
                        out=xp_d[0:128, t0 + 512 * tt:t0 + 512 * (tt + 1)],
                        in_=xpo[:, :]), 16)
                    sGw.inc(GP.dma_start(
                        out=xp_d[128:160, t0 + 512 * tt:t0 + 512 * (tt + 1)],
                        in_=xpo2[:, :]), 16)
        pf.close()

    emit_ffn(x1d, x2d, first_ffn=True)
    GP.wait_ge(sGw.h, sGw.n)
    GP.wait_ge(sK.h, relay())
    sGw.inc(GP.dma_start(
        out=shm_xn.rearrange("s d t -> (s d) t")[bass.ds(gp_xn_base, D), :],
        in_=xn3d[:, :]), 16)
    sGw.inc(GP.dma_start(
        out=shm_xp.rearrange("s p t -> (s p) t")[bass.ds(gp_xp_lo, 160), :],
        in_=xp_d[:, :]), 16)
    GP.wait_ge(sGw.h, sGw.n)
    flag_write(FLAG_XN)

    if upto == "f1":
        if dbg:
            dbg_cmb = nc.dram_tensor("dbg_cmb", [E, TLOC], F32, kind="ExternalOutput").ap()
            dbg_moe = nc.dram_tensor("dbg_moe", [D, TG], F32, kind="ExternalOutput").ap()
            with nc.sbuf_tensor("dbg_cb", [E, TLOC], F32) as dcb:
                SY.wait_ge(sS.h, sS.n)
                q1 = sQx[0].inc(SY.dma_start(out=dcb[:, :], in_=cmbd), 16)
                SY.wait_ge(sQx[0].h, q1)
                sS.inc(SY.dma_start(out=dbg_cmb, in_=dcb[:, :]), 16)
                SY.wait_ge(sS.h, sS.n)
            with nc.sbuf_tensor("dbg_my", [128, DC, 512], F32) as dmy:
                for tt in range(2):
                    q1 = sQx[0].inc(SY.dma_start(out=dmy[:, :, :],
                                                 in_=fm(x3d, 512 * tt, 512)), 16)
                    SY.wait_ge(sQx[0].h, q1)
                    sS.inc(SY.dma_start(out=fm(dbg_moe, 512 * tt, 512),
                                        in_=dmy[:, :, :]), 16)
                    SY.wait_ge(sS.h, sS.n)
        return copy_out(x2d)

    c_xn = flag_poll(FLAG_XN, rows=(0, 1))
    pc = ExitStack()
    CCx = pc.enter_context
    bcb = CCx(nc.sbuf_tensor("bcb", [128, 32, TCH], F32))
    dlow = CCx(nc.sbuf_tensor("dlow", [128, 2, TCH], BF16))
    xrow = CCx(nc.sbuf_tensor("xrow", [128, 2, TCH], F32))
    dtt = CCx(nc.sbuf_tensor("dtt", [128, TCH], F32))
    spt = CCx(nc.sbuf_tensor("spt", [128, TCH], F32))
    dtx = CCx(nc.sbuf_tensor("dtx", [128, TCH], F32))
    abar = CCx(nc.sbuf_tensor("abar", [128, TCH, NS], F32))
    bxh = CCx(nc.sbuf_tensor("bxh", [128, TCH, NS], F32))
    hsc = CCx(nc.sbuf_tensor("hsc", [128, TCH, NS], F32))
    ysc = CCx(nc.sbuf_tensor("ysc", [128, 2, TCH, 1], F32))
    carr = CCx(nc.sbuf_tensor("carr", [128, DTI, NS], F32))

    SY.wait_ge(sC.h, c_xn)
    GP.wait_ge(sC.h, c_xn)
    VE.wait_ge(sC.h, c_xn)
    VE.wait_ge(sV.h, v_acl)
    for Hh in range(2):
        sQx[0].inc(SY.dma_start(
            out=xloc[:, TLOC * Hh:TLOC * (Hh + 1)],
            in_=shm_xn[Hh][bass.ds(sy_1024, 1024), :]), 16)
    q_xloc = sQx[0].n
    SY.wait_ge(sQx[0].h, q_xloc)
    load_act_set(nc, SET_LNEXP)
    ymk = {}
    carr_mark = {}
    for ch in range(NCH):
        H = ch // 8
        cc = 512 * (ch % 8)
        GP.wait_ge(sV.h, sV.n)
        GP.wait_ge(sP.h, sP.n)
        g_bc = sGx.inc(GP.dma_start(
            out=bcb[:, :, :],
            in_=shm_xp[H, 128:160, cc:cc + TCH].unsqueeze(0).broadcast_to(
                [128, 32, TCH])), 16)
        g_dl = sGx.inc(GP.dma_start(out=dlow[:, ch % 2, :],
                                    in_=shm_xp[H, 0:128, cc:cc + TCH]), 16)
        for i in range(DTI):
            dbank = pb[i % 2]
            if (ch, i) >= (0, 2):
                SY.wait_ge(sV.h, sV.n)
            q_x = sQx[i % 2].inc(SY.dma_start(
                out=xrow[:, i % 2, :],
                in_=xloc[128 * i:128 * (i + 1), TLOC * H + cc:TLOC * H + cc + TCH]), 16)
            q_x = (i % 2, q_x)
            TE.wait_ge(sGx.h, g_dl)
            TE.wait_ge(sA.h, sA.n)
            p_dt = sP.inc(TE.matmul(dbank[:, 0:512], sb_dtw[:, i, :],
                                    dlow[:, ch % 2, :], start=True, stop=True))
            SC.wait_ge(sP.h, p_dt)
            SC.wait_ge(sV.h, sV.n)
            SC.activation(spt[:, :], dbank[:, 0:512], AF.Abs, bias=sb_dtb[:, i:i + 1])
            SC.activation(spt[:, :], spt[:, :], AF.Exp, scale=-1.0)
            SC.activation(spt[:, :], spt[:, :], AF.Ln, bias=1.0)
            a_sp = sA.inc(SC.activation(dtt[:, :], dbank[:, 0:512], AF.Relu,
                                        bias=sb_dtb[:, i:i + 1]))
            VE.wait_ge(sA.h, a_sp)
            v_dt = sV.inc(VE.tensor_add(dtt[:, :], dtt[:, :], spt[:, :]))
            SC.wait_ge(sV.h, v_dt)
            for n in range(NS):
                am = SC.activation(abar[:, :, n], dtt[:, :], AF.Exp,
                                   scale=sb_acl[:, i, n:n + 1])
            a_ab = sA.inc(am)
            VE.wait_ge(sQx[q_x[0]].h, q_x[1])
            VE.tensor_mul(dtx[:, :], dtt[:, :], xrow[:, i % 2, :])
            VE.wait_ge(sGx.h, g_bc)
            VE.tensor_tensor(bxh[:, :, :],
                             dtx[:, :].unsqueeze(2).broadcast_to([128, TCH, NS]),
                             bcb[:, 0:NS, :].transpose([0, 2, 1]), OP.mult)
            VE.wait_ge(sA.h, a_ab)
            if ch > 0:
                VE.wait_ge(sC.h, carr_mark[i])
            for n in range(NS):
                init = 0.0 if ch == 0 else carr[:, i, n:n + 1]
                VE.tensor_tensor_scan(hsc[:, :, n], abar[:, :, n], bxh[:, :, n],
                                      init, OP.mult, OP.add)
            v_hs = sV.inc(VE.tensor_copy(dtt[:, 0:1], dtt[:, 0:1]))
            GP.wait_ge(sV.h, v_hs)
            carr_mark[i] = sC.inc(GP.tensor_copy(carr[:, i, :], hsc[:, TCH - 1, :]))
            VE.tensor_tensor(bxh[:, :, :], hsc[:, :, :],
                             bcb[:, NS:32, :].transpose([0, 2, 1]), OP.mult)
            ybuf2 = (ch * DTI + i) % 2
            if (ch * DTI + i) >= 2:
                VE.wait_ge(sS.h, ymk[(ch * DTI + i) - 2])
            v_y = sV.inc(VE.tensor_reduce(ysc[:, ybuf2, :, :], bxh[:, :, :], AX.X, OP.add))
            SY.wait_ge(sV.h, v_y)
            ymk[ch * DTI + i] = sS.inc(SY.dma_start(
                out=yloc1[128 * i:128 * (i + 1), TLOC * H + cc:TLOC * H + cc + TCH],
                in_=ysc[:, ybuf2, :, 0]), 16)
    pc.close()
    SY.wait_ge(sS.h, sS.n)
    for Hh in range(2):
        sS.inc(SY.dma_start(
            out=shm_y[Hh].rearrange("s d t -> (s d) t")[bass.ds(sy_1024, 1024), :],
            in_=yloc1[:, TLOC * Hh:TLOC * (Hh + 1)]), 16)
    GP.wait_ge(sK.h, relay())
    flag_write(FLAG_Y)
    c_y = flag_poll(FLAG_Y, rows=(0, 1))

    pd = ExitStack()
    DDx = pd.enter_context
    ybf = DDx(nc.sbuf_tensor("ybf", [128, 2, 512], F32))
    nbf = DDx(nc.sbuf_tensor("nbf", [128, 2, 512], F32))
    obf = DDx(nc.sbuf_tensor("obf", [128, 2, 512], F32))
    SY.wait_ge(sC.h, c_y)
    VE.wait_ge(sC.h, c_y)
    sQx[0].inc(SY.dma_start(
        out=yloc2[:, :],
        in_=shm_y.rearrange("h s d t -> (h s d) t")[bass.ds(sy_2048, 2048), :]), 16)
    SY.wait_ge(sQx[0].h, sQx[0].n)
    dmk = {}
    idx = 0
    for t8 in range(8):
        c0 = 512 * t8
        for k in range(DC):
            buf = idx % 2
            if idx >= 2:
                SY.wait_ge(sV.h, dmk[idx - 2])
            sQx[buf].inc(SY.dma_start(
                out=ybf[:, buf, :],
                in_=yloc2[1024 * (k // 8) + 128 * (k % 8):
                          1024 * (k // 8) + 128 * (k % 8) + 128, c0:c0 + 512]), 16)
            sQx[buf].inc(SY.dma_start(
                out=nbf[:, buf, :],
                in_=xn3d[128 * k:128 * (k + 1), c0:c0 + 512]), 16)
            q_x2 = sQx[buf].inc(SY.dma_start(
                out=obf[:, buf, :],
                in_=x2d[128 * k:128 * (k + 1), c0:c0 + 512]), 16)
            VE.wait_ge(sQx[buf].h, q_x2)
            VE.scalar_tensor_tensor(nbf[:, buf, :], nbf[:, buf, :],
                                    sb_dpm[:, k:k + 1], ybf[:, buf, :],
                                    OP.mult, OP.add)
            dmk[idx] = sV.inc(VE.tensor_add(obf[:, buf, :], obf[:, buf, :],
                                            nbf[:, buf, :]))
            SY.wait_ge(sV.h, dmk[idx])
            sS.inc(SY.dma_start(out=x3d[128 * k:128 * (k + 1), c0:c0 + 512],
                                in_=obf[:, buf, :]), 16)
            idx += 1
    pd.close()
    if upto == "b":
        return copy_out(x3d)

    emit_ffn(x3d, outT, first_ffn=False)

    SY.wait_ge(sS.h, sS.n)
    GP.wait_ge(sK.h, relay())
    es.close()
    return nc


def _prep(inputs):
    c = np.ascontiguousarray
    f = np.float32
    x = np.asarray(inputs["L0_attn_a_x"], f)
    xt = c(x.reshape(B * L, D).T)

    gamma = np.asarray(inputs["L0_attn_a_ln_gamma"], f)
    beta = np.asarray(inputs["L0_attn_a_ln_beta"], f)
    W_in = np.asarray(inputs["L0_attn_a_W_in"], f)
    b_in = np.asarray(inputs["L0_attn_a_b_in"], f)
    dec_l = np.asarray(inputs["L0_attn_a_decay_logit"], f)
    W_out = np.asarray(inputs["L0_attn_a_W_out"], f)
    b_out = np.asarray(inputs["L0_attn_a_b_out"], f)
    D_skip = np.asarray(inputs["L0_attn_a_D_skip"], f)
    W_down = np.asarray(inputs["L0_attn_a_W_down"], f)
    b_down = np.asarray(inputs["L0_attn_a_b_down"], f)
    W_up = np.asarray(inputs["L0_attn_a_W_up"], f)
    b_up = np.asarray(inputs["L0_attn_a_b_up"], f)
    xpw = np.asarray(inputs["L0_attn_b_x_proj_w"], f)
    dtw = np.asarray(inputs["L0_attn_b_dt_proj_w"], f)
    dtb = np.asarray(inputs["L0_attn_b_dt_proj_b"], f)
    A_log = np.asarray(inputs["L0_attn_b_A_log"], f)
    D_param = np.asarray(inputs["L0_attn_b_D_param"], f)
    gate_w = np.asarray(inputs["L0_ffn_gate_w"], f)
    w1 = np.asarray(inputs["L0_ffn_w1"], f)
    w2 = np.asarray(inputs["L0_ffn_w2"], f)
    sw1 = np.asarray(inputs["L0_ffn_shared_w1"], f)
    sw2 = np.asarray(inputs["L0_ffn_shared_w2"], f)

    cm = lambda v: c(v.reshape(DC, 128).T)

    w1v = w1.reshape(E, 2 * I1, DC, 128)
    w1g = w1v[:, :I1].reshape(E, NJ, 128, DC, 128).transpose(0, 1, 4, 3, 2)
    w1u = w1v[:, I1:].reshape(E, NJ, 128, DC, 128).transpose(0, 1, 4, 3, 2)
    w1t = c(np.concatenate([w1g, w1u], axis=-1))
    w2t = c(w2.reshape(E, DC, 128, NJ, 128).transpose(0, 1, 4, 3, 2))
    s1g = sw1[:IS].reshape(NJS, 128, DC, 128).transpose(0, 3, 2, 1)
    s1u = sw1[IS:].reshape(NJS, 128, DC, 128).transpose(0, 3, 2, 1)
    sw1t = c(np.concatenate([s1g, s1u], axis=-1))
    sw2t = c(sw2.reshape(DC, 128, NJS, 128).transpose(0, 3, 2, 1))

    shared = {
        "gam": cm(gamma), "bet": cm(beta),
        "winA": c(W_in.reshape(DC, 128, RA)),
        "b_in": c(b_in.reshape(RA, 1)),
        "dec_l": c(dec_l.reshape(RA, 1)),
        "woutA": c(np.concatenate([W_out, b_out[None, :]], 0)),
        "dskip": cm(D_skip),
        "wdnA": c(W_down.reshape(DC, 128, RF)),
        "b_dn": c(b_down.reshape(RF, 1)),
        "wupA": c(np.concatenate([W_up, b_up[None, :]], 0)),
        "xpwT": c(xpw.T.reshape(DC, 128, 160)),
        "dtwT": c(dtw.T),
        "dtbT": cm(dtb),
        "alog": c(A_log),
        "dparm": cm(D_param),
        "gwT": c(gate_w.T.reshape(DC, 128, E)),
        "w1t": w1t, "w2t": w2t, "sw1t": sw1t, "sw2t": sw2t,
    }
    nonce = int(np.random.randint(1, 2**30))
    in_maps = []
    for cc_ in range(NC):
        m = dict(shared)
        m["xT"] = c(xt[:, TLOC * cc_:TLOC * (cc_ + 1)])
        m["meta"] = np.array([[cc_, cc_ % 2, cc_ // 2, nonce, 0, 0, 0, 0]], np.int32)
        m["pmask8"] = np.full((RA, 1), float(cc_ % 2), np.float32)
        in_maps.append(m)
    return in_maps


def kernel(**inputs):
    in_maps = _prep(inputs)
    nc = build(upto="full")
    res = run_bass_kernel_spmd(nc, in_maps, core_ids=list(range(NC)))
    outs = [np.asarray(res.results[c]["outT"]) for c in range(NC)]
    full = np.concatenate(outs, axis=1)
    return np.ascontiguousarray(full.T).reshape(B, L, D).astype(np.float32)
